# version 09
# baseline (speedup 1.0000x reference)
import contextlib
import numpy as np
import ml_dtypes

import concourse.bacc as bacc
import concourse.bass as bass
import concourse.mybir as mybir
from concourse.bass_utils import run_bass_kernel_spmd

F32 = mybir.dt.float32
BF16 = mybir.dt.bfloat16
Alu = mybir.AluOpType
Act = mybir.ActivationFunctionType
AxX = mybir.AxisListType.X

B, L, HID = 2, 4096, 4096
H, KVH, D, CHUNK = 32, 8, 128, 128
I_FFN = 11008
N_CORES = 8
N_TOK = 1024
NCH = 8
ROPE_THETA = 10000.0
RMS_EPS = 1e-5
LN_EPS = 1e-5
MAGIC = 12582912.0
QEPS = 1e-8
NEG_BIG = -1e30
NCC = H * D * D + 32

ENGS = ("sync", "scalar", "vector", "tensor", "gpsimd")


class Plan:

    def __init__(self, nc, name):
        self.nc = nc
        self.name = name
        self.q = {e: [] for e in ENGS}

    def on(self, eng):
        def deco(fn):
            self.q[eng].append(fn)
            return None
        return deco

    def sem(self, name):
        return Sem(self.nc, f"{self.name}_{name}")

    def run(self):
        with self.nc.Block(self.name) as blk:
            for eng in ENGS:
                thunks = self.q[eng]
                if thunks:
                    getattr(blk, eng)(lambda e, th=tuple(thunks): [f(e) for f in th])


class Sem:
    def __init__(self, nc, name):
        self.h = nc.alloc_semaphore(name)
        self.v = 0

    def take(self, n=1):
        self.v += n
        return self.v

    def take_dma(self):
        prev = self.v
        self.v += 16
        return prev, self.v


def ts(e, out, in0, s1, op0, s2=None, op1=Alu.bypass):
    return e.tensor_scalar(out, in0, s1, s2, op0, op1)



def _host_consts(core):
    q = core % 4
    t0 = q * N_TOK
    pos = np.arange(t0, t0 + N_TOK, dtype=np.float32)
    inv_freq = 1.0 / ROPE_THETA ** (np.arange(0, D, 2, dtype=np.float32) / D)
    ang = pos[:, None] * inv_freq
    cos = np.concatenate([np.cos(ang), np.cos(ang)], axis=1).astype(np.float32)
    sin = np.concatenate([np.sin(ang), np.sin(ang)], axis=1).astype(np.float32)
    sinS = sin.copy()
    sinS[:, :64] *= -1.0
    cos_tok = np.ascontiguousarray(
        cos.reshape(NCH, 128, D).transpose(1, 0, 2).reshape(128, NCH * D))
    sinS_tok = np.ascontiguousarray(
        sinS.reshape(NCH, 128, D).transpose(1, 0, 2).reshape(128, NCH * D))
    jj, ii = np.meshgrid(np.arange(128), np.arange(128), indexing="ij")
    maskT = np.where(jj <= ii, 0.0, NEG_BIG).astype(np.float32)
    ident = np.eye(128, dtype=np.float32)
    ones_row = np.ones((1, 128), dtype=np.float32)
    e_cols = np.zeros((128, 16), np.float32)
    b_tile = np.full((128, 128), NEG_BIG, np.float32)
    for r in range(4):
        if r < q:
            b_tile[:, r * 32:(r + 1) * 32] = 0.0
            for rp in range(r + 1, q):
                e_cols[:, r * 4 + rp] = 1.0
    return dict(cos_c=cos_tok, sinS_c=sinS_tok, mask_c=maskT, ident_c=ident,
                ones_c=ones_row, ecols_c=e_cols, btile_c=b_tile)



def build_program(depth_stop=99, debug=0):
    nc = bacc.Bacc()
    E = {"debug": debug}

    def dram(name, shape, dt=F32, kind="Internal"):
        E[name] = nc.dram_tensor(name, shape, dt, kind=kind)
        return E[name]

    for nm, shape in [
        ("xT", [HID, N_TOK]), ("x_tok", [N_TOK, HID]),
        ("WqT", [HID, H * D]), ("WkT", [HID, KVH * D]), ("WvT", [HID, KVH * D]),
        ("WgT", [HID, H * D]), ("WoT", [H * D, HID]), ("WinT", [HID, H]),
        ("Wg_f", [I_FFN, HID]), ("Wu_f", [I_FFN, HID]), ("Wd_f", [HID, I_FFN]),
        ("dtb_t", [128, H]), ("alog_t", [128, H]), ("gnorm_t", [128, D]),
        ("cos_c", [128, NCH * D]), ("sinS_c", [128, NCH * D]),
        ("mask_c", [128, 128]), ("ident_c", [128, 128]), ("ones_c", [1, 128]),
        ("ecols_c", [128, 16]), ("btile_c", [128, 128]),
    ]:
        dram(nm, shape, F32, "ExternalInput")
    dram("out", [N_TOK, HID], F32, "ExternalOutput")
    scrkind = "ExternalOutput" if debug >= 3 else "Internal"
    dram("r1_scr", [N_TOK, HID], F32, scrkind)
    dram("r2_scr", [N_TOK, HID], F32, scrkind)
    dram("hq_scr", [N_TOK, I_FFN], BF16, scrkind)
    dram("wgq_scr", [I_FFN, HID], BF16, scrkind)
    dram("wuq_scr", [I_FFN, HID], BF16, scrkind)
    dram("wdq_scr", [HID, I_FFN], BF16, scrkind)
    dram("gl_scr", [1, NCH * H], F32)
    dram("cc_in", [1, NCC], F32)
    dram("cc_out", [4, NCC], F32)

    A = nc.alloc_sbuf_tensor
    for nm, shape, dt in [
        ("cos_s", [128, NCH * D], F32), ("sinS_s", [128, NCH * D], F32),
        ("mask_s", [128, 128], F32), ("ident_s", [128, 128], F32),
        ("ones_s", [1, 128], F32), ("gnorm_s", [128, D], F32),
        ("dtb_s", [128, H], F32), ("aneg_s", [128, H], F32),
        ("ecols_s", [128, 16], F32), ("btile_s", [128, 128], F32),
        ("st_sum", [128, NCH], F32), ("st_sq", [128, NCH], F32),
        ("ln1_mean", [128, NCH], F32), ("ln1_inv", [128, NCH], F32),
        ("actc", [128, 6], F32),
    ]:
        E[nm] = A(nm, shape, dt)
    E["PS"] = [nc.alloc_psum_tensor(f"ps{i}", [128, 512], F32) for i in range(8)]

    dbg = {}
    if debug >= 1:
        for nm, shape in [("dbg_G", [128, NCH * H]), ("dbg_state0", [128, H * D]),
                          ("dbg_ktok", [128, NCH * KVH * D]), ("dbg_vtok", [128, NCH * KVH * D]),
                          ("dbg_dt", [128, NCH * H])]:
            dbg[nm] = nc.dram_tensor(nm, shape, F32, kind="ExternalOutput")
    E["dbg"] = dbg

    _p0(nc, E)
    if depth_stop >= 1:
        with (
            nc.sbuf_tensor("k_tok", [128, NCH * KVH * D], BF16) as k_tok,
            nc.sbuf_tensor("v_tok", [128, NCH * KVH * D], BF16) as v_tok,
            nc.sbuf_tensor("dt_tok", [128, NCH * H], F32) as dt_tok,
            nc.sbuf_tensor("G_tok", [128, NCH * H], F32) as G_tok,
            nc.sbuf_tensor("G_fm", [32, NCH * 128], F32) as G_fm,
            nc.sbuf_tensor("kdec_e", [128, NCH * H], F32) as kdec_e,
            nc.sbuf_tensor("cdall", [128, NCH * H], F32) as cdall,
            nc.sbuf_tensor("state", [128, H * D], F32) as state,
            nc.sbuf_tensor("glrow", [1, NCH * H], F32) as glrow,
        ):
            E.update(k_tok=k_tok, v_tok=v_tok, dt_tok=dt_tok, G_tok=G_tok,
                     G_fm=G_fm, kdec_e=kdec_e, cdall=cdall, state=state,
                     glrow=glrow)
            _p1(nc, E)
            if depth_stop >= 2:
                _exchange(nc, E)
            if depth_stop >= 3:
                for half in (0, 1):
                    _p2_half(nc, E, half)
            if debug >= 1:
                _dbg_dump(nc, E)
    if depth_stop >= 4:
        with nc.sbuf_tensor("xq_fm", [128, 32 * N_TOK], BF16) as xq_fm:
            E["xq_fm"] = xq_fm
            _p25_ln1(nc, E)
            if depth_stop >= 5:
                _p3_ffn_gu(nc, E)
        if depth_stop >= 6:
            _p4_down(nc, E)
    return nc, E



def _p0(nc, E):
    p = Plan(nc, "p0")
    s_ld = p.sem("ld")
    ldv = s_ld.take(160)

    @p.on("sync")
    def _(e):
        for i, (dst, src) in enumerate([("cos_s", "cos_c"), ("sinS_s", "sinS_c"), ("mask_s", "mask_c"),
                         ("ident_s", "ident_c"), ("ones_s", "ones_c"), ("gnorm_s", "gnorm_t"),
                         ("dtb_s", "dtb_t"), ("aneg_s", "alog_t"), ("ecols_s", "ecols_c"),
                         ("btile_s", "btile_c")]):
            if i:
                e.wait_ge(s_ld.h, 16 * i)
            e.dma_start(out=E[dst][:, :], in_=E[src][:, :]).then_inc(s_ld.h, 16)

    @p.on("scalar")
    def _(e):
        e.wait_ge(s_ld.h, ldv)
        e.activation(E["aneg_s"][:, :], E["aneg_s"][:, :], Act.Exp)
        e.drain()
        e.mul(E["aneg_s"][:, :], E["aneg_s"][:, :], -1.0)

    @p.on("vector")
    def _(e):
        e.wait_ge(s_ld.h, ldv)
        e.memset(E["st_sum"][:, :], 0.0)
        e.memset(E["st_sq"][:, :], 0.0)
        e.memset(E["actc"][:, 0:1], RMS_EPS)
        e.memset(E["actc"][:, 1:2], LN_EPS)
        e.memset(E["actc"][:, 2:3], 1.0 / 128.0)
        e.memset(E["actc"][:, 3:4], 1.0 / HID)
        e.memset(E["actc"][:, 4:5], MAGIC)
        e.memset(E["actc"][:, 5:6], -MAGIC)

    p.run()



def _p1(nc, E):
    with (
        nc.sbuf_tensor("p1_xfm", [128, 32 * 512], BF16) as xfm,
        nc.sbuf_tensor("p1_wt", [128, 2 * 8 * 512], BF16) as wt,
        nc.sbuf_tensor("p1_zero32", [32, 128], F32) as zero32,
        nc.sbuf_tensor("p1_ztmp", [128, 32], F32) as ztmp,
        nc.sbuf_tensor("p1_sp", [128, 4 * 32], F32) as sp,
        nc.sbuf_tensor("p1_gla", [128, NCH * H], F32) as glast_b,
        nc.sbuf_tensor("p1_rt", [128, 2 * 128], F32) as rtmp,
        nc.sbuf_tensor("p1_xc", [128, 8 * 128], BF16) as xc_t,
        nc.sbuf_tensor("p1_kd", [128, 8 * 128], BF16) as kd_t,
    ):
        p = Plan(nc, "p1")
        S = dict(x=p.sem("x"), w=[p.sem("w0"), p.sem("w1")], mm=p.sem("mm"), ev=p.sem("ev"),
                 evs=p.sem("evs"), gv=p.sem("gv"), gt=p.sem("gt"), z=p.sem("z"),
                 pre=p.sem("pre"), cs=p.sem("cs"), up=p.sem("up"), gd=p.sem("gd"))
        bufs = dict(xfm=xfm, wt=wt, zero32=zero32, ztmp=ztmp, sp=sp,
                    glast_b=glast_b, rtmp=rtmp, xc_t=xc_t, kd_t=kd_t)

        @p.on("vector")
        def _(e):
            e.memset(zero32[:, :], 0.0)
            e.memset(E["state"][:, :], 0.0)
            e.drain()

        os_plan = [("k", "WkT", 0, 512), ("k", "WkT", 512, 512),
                   ("v", "WvT", 0, 512), ("v", "WvT", 512, 512),
                   ("win", "WinT", 0, 32)]
        gi_ref = [0]
        for half in (0, 1):
            _p1_load_xfm(p, E, S, bufs, half)
            for oi, unit in enumerate(os_plan):
                _p1_proj_os(p, E, S, bufs, half, oi, unit)
            _p1_glast_cs(p, E, S, bufs, half, gi_ref)
        p.run()


def _p1_load_xfm(p, E, S, bufs, half):
    xfm = bufs["xfm"]
    mm_before = S["mm"].v
    xprev, _ = S["x"].take_dma()

    @p.on("gpsimd")
    def _(e):
        if half == 1:
            e.wait_ge(S["mm"].h, mm_before)
        if xprev:
            e.wait_ge(S["x"].h, xprev)
        ap_in = bass.AP(E["xT"], half * 512, [[N_TOK, 128], [N_TOK * 128, 32], [1, 512]])
        e.dma_start(out=xfm[:, :].rearrange("p (a b) -> p a b", a=32),
                    in_=ap_in).then_inc(S["x"].h, 16)


def _p1_proj_os(p, E, S, bufs, half, oi, unit):
    kind, wdn, c0, wid = unit
    PS = E["PS"]
    xfm, wt = bufs["xfm"], bufs["wt"]
    wd = E[wdn]
    wcols = KVH * D if kind != "win" else H
    ev_before, evs_before = S["ev"].v, S["evs"].v
    xv = S["x"].v
    gidx0 = (half * 5 + oi) * 4

    for ig in range(4):
        gidx = gidx0 + ig
        par = gidx % 2
        wprev, wv = S["w"][par].take_dma()

        @p.on("gpsimd")
        def _(e, ig=ig, gidx=gidx, par=par, wprev=wprev):
            if gidx >= 2:
                e.wait_ge(S["mm"].h, gidx - 1)
            if wprev:
                e.wait_ge(S["w"][par].h, wprev)
            buf = par * 8
            ap_w = bass.AP(wd, (ig * 1024) * wcols + c0,
                           [[wcols, 128], [wcols * 128, 8], [1, wid]])
            dst = wt[:, buf * 512:(buf + 8) * 512].rearrange("p (a b) -> p a b", a=8)
            e.dma_start(out=dst[:, :, 0:wid], in_=ap_w).then_inc(S["w"][par].h, 16)

        S["mm"].take(1)

        @p.on("tensor")
        def _(e, ig=ig, gidx=gidx, par=par, wv=wv):
            if ig == 0:
                e.wait_ge(S["x"].h, xv)
                e.wait_ge(S["ev"].h, ev_before)
                e.wait_ge(S["evs"].h, evs_before)
            e.wait_ge(S["w"][par].h, wv)
            buf = (gidx % 2) * 8
            last = None
            for irel in range(8):
                i = ig * 8 + irel
                for c in range(4):
                    last = e.matmul(PS[c][:, 0:wid],
                                    xfm[:, i * 512 + c * 128: i * 512 + c * 128 + 128],
                                    wt[:, (buf + irel) * 512:(buf + irel) * 512 + wid],
                                    start=(i == 0), stop=(i == 31))
            last.then_inc(S["mm"].h, 1)

    mm_done = S["mm"].v

    if kind == "k":
        S["ev"].take(1)

        @p.on("vector")
        def _(e):
            e.wait_ge(S["mm"].h, mm_done)
            rtmp = bufs["rtmp"]
            for cc in range(4):
                c = half * 4 + cc
                for kv in range(4):
                    kvh = c0 // 128 + kv
                    psl = PS[cc][:, kv * 128:(kv + 1) * 128]
                    cosl = E["cos_s"][:, c * 128:(c + 1) * 128]
                    e.drain()
                    e.tensor_tensor(rtmp[:, 0:128], psl, cosl, Alu.mult)
                    e.tensor_tensor(rtmp[:, 128:192],
                                    PS[cc][:, kv * 128 + 64:kv * 128 + 128],
                                    E["sinS_s"][:, c * 128:c * 128 + 64], Alu.mult)
                    e.tensor_tensor(rtmp[:, 192:256], PS[cc][:, kv * 128:kv * 128 + 64],
                                    E["sinS_s"][:, c * 128 + 64:c * 128 + 128], Alu.mult)
                    e.drain()
                    e.tensor_tensor(
                        E["k_tok"][:, (c * KVH + kvh) * 128:(c * KVH + kvh + 1) * 128],
                        rtmp[:, 0:128], rtmp[:, 128:256], Alu.add)
            e.drain().then_inc(S["ev"].h, 1)
    elif kind == "v":
        S["evs"].take(1)

        @p.on("scalar")
        def _(e):
            e.wait_ge(S["mm"].h, mm_done)
            for cc in range(4):
                c = half * 4 + cc
                for kv in range(4):
                    kvh = c0 // 128 + kv
                    e.copy(E["v_tok"][:, (c * KVH + kvh) * 128:(c * KVH + kvh + 1) * 128],
                           PS[cc][:, kv * 128:(kv + 1) * 128])
            e.drain().then_inc(S["evs"].h, 1)
    else:
        _p1_win_evac(p, E, S, bufs, half, mm_done)


def _p1_win_evac(p, E, S, bufs, half, mm_done):
    PS = E["PS"]
    ztmp, sp, zero32 = bufs["ztmp"], bufs["sp"], bufs["zero32"]
    for cc in range(4):
        c = half * 4 + cc
        zv = S["z"].take(1)

        @p.on("vector")
        def _(e, cc=cc, c=c):
            if cc == 0:
                e.wait_ge(S["mm"].h, mm_done)
            e.drain()
            e.tensor_tensor(ztmp[:, :], PS[cc][:, 0:32], E["dtb_s"][:, :], Alu.add)
            e.drain()
            e.tensor_relu(sp[:, 96:128], ztmp[:, :])
            e.drain().then_inc(S["z"].h, 1)

        zv2 = S["z"].take(1)

        @p.on("scalar")
        def _(e, zv=zv):
            e.wait_ge(S["z"].h, zv)
            e.activation(sp[:, 0:32], ztmp[:, :], Act.Abs)
            e.drain()
            e.activation(sp[:, 32:64], sp[:, 0:32], Act.Exp, scale=-1.0)
            e.drain()
            e.activation(sp[:, 64:96], sp[:, 32:64], Act.Ln, bias=1.0)
            e.drain().then_inc(S["z"].h, 1)

        zv3 = S["z"].take(1)

        @p.on("vector")
        def _(e, c=c, zv2=zv2):
            e.wait_ge(S["z"].h, zv2)
            e.drain()
            e.tensor_tensor(E["dt_tok"][:, c * H:(c + 1) * H], sp[:, 64:96],
                            sp[:, 96:128], Alu.add)
            e.drain()
            e.tensor_tensor(ztmp[:, :], E["dt_tok"][:, c * H:(c + 1) * H],
                            E["aneg_s"][:, :], Alu.mult)
            e.drain().then_inc(S["z"].h, 1)

        gtv = S["gt"].take(1)

        @p.on("tensor")
        def _(e, zv3=zv3):
            e.wait_ge(S["z"].h, zv3)
            e.transpose(PS[4][0:32, 0:128], ztmp[:, :],
                        E["ident_s"][:, :]).then_inc(S["gt"].h, 1)

        gvv = S["gv"].take(1)

        @p.on("vector")
        def _(e, c=c, gtv=gtv):
            e.wait_ge(S["gt"].h, gtv)
            e.drain()
            e.tensor_tensor_scan(E["G_fm"][:, c * 128:(c + 1) * 128],
                                 PS[4][0:32, 0:128], zero32[:, :], 0.0,
                                 Alu.add, Alu.add)
            e.drain().then_inc(S["gv"].h, 1)

        gtv2 = S["gt"].take(1)

        @p.on("tensor")
        def _(e, c=c, gvv=gvv):
            e.wait_ge(S["gv"].h, gvv)
            e.transpose(PS[5][:, 0:32], E["G_fm"][0:32, c * 128:(c + 1) * 128],
                        E["ident_s"][0:32, 0:32]).then_inc(S["gt"].h, 1)

        last = (cc == 3)
        if last:
            S["ev"].take(1)
            S["evs"].take(1)

        @p.on("vector")
        def _(e, c=c, gtv2=gtv2, last=last):
            e.wait_ge(S["gt"].h, gtv2)
            e.drain()
            e.tensor_copy(E["G_tok"][:, c * H:(c + 1) * H], PS[5][:, 0:32])
            e.drain()
            if last:
                e.drain().then_inc(S["ev"].h, 1)
                e.drain().then_inc(S["evs"].h, 1)


def _p1_glast_cs(p, E, S, bufs, half, gi_ref):
    PS = E["PS"]
    glast_b, xc_t, kd_t, zero32 = bufs["glast_b"], bufs["xc_t"], bufs["kd_t"], bufs["zero32"]
    h0 = half * 128
    ev_now, evs_now = S["ev"].v, S["evs"].v
    gd0, gd1 = S["gd"].take_dma()
    _, gd2 = S["gd"].take_dma()

    @p.on("sync")
    def _(e):
        e.wait_ge(S["ev"].h, ev_now)
        e.wait_ge(S["evs"].h, evs_now)
        if gd0:
            e.wait_ge(S["gd"].h, gd0)
        e.dma_start(out=bass.AP(E["gl_scr"], h0, [[0, 1], [1, 128]]),
                    in_=E["G_tok"][127:128, h0:h0 + 128]).then_inc(S["gd"].h, 16)
        e.wait_ge(S["gd"].h, gd1)
        e.dma_start(out=E["glrow"][0:1, h0:h0 + 128],
                    in_=bass.AP(E["gl_scr"], h0, [[0, 1], [1, 128]])).then_inc(S["gd"].h, 16)

    gtv = S["gt"].take(1)

    @p.on("tensor")
    def _(e):
        e.wait_ge(S["gd"].h, gd2)
        e.matmul(PS[4][:, 0:128], E["ones_s"][:, :],
                 E["glrow"][0:1, h0:h0 + 128], start=True,
                 stop=True).then_inc(S["gt"].h, 1)

    gvv = S["gv"].take(1)

    @p.on("vector")
    def _(e, gtv=gtv):
        e.wait_ge(S["gt"].h, gtv)
        e.drain()
        e.tensor_copy(glast_b[:, h0:h0 + 128], PS[4][:, 0:128])
        e.drain()
        e.tensor_sub(E["kdec_e"][:, h0:h0 + 128], glast_b[:, h0:h0 + 128],
                     E["G_tok"][:, h0:h0 + 128])
        e.drain().then_inc(S["gv"].h, 1)

    decv = S["gv"].take(1)

    @p.on("scalar")
    def _(e, gvv=gvv):
        e.wait_ge(S["gv"].h, gvv)
        e.activation(E["kdec_e"][:, h0:h0 + 128], E["kdec_e"][:, h0:h0 + 128], Act.Exp)
        e.activation(E["cdall"][:, h0:h0 + 128], glast_b[:, h0:h0 + 128], Act.Exp)
        e.drain().then_inc(S["gv"].h, 1)

    for cc in range(4):
        c = half * 4 + cc
        for hgrp in range(8):
            gi = gi_ref[0]
            gi_ref[0] += 1
            prev = S["pre"].take(1)

            @p.on("vector")
            def _(e, c=c, cc=cc, hgrp=hgrp, gi=gi):
                if hgrp == 0 and cc == 0:
                    e.wait_ge(S["gv"].h, decv)
                if gi >= 2:
                    e.wait_ge(S["cs"].h, gi - 1)
                bo = (gi % 2) * 4
                e.drain()
                for hh in range(4):
                    h = hgrp * 4 + hh
                    col = c * H + h
                    ts(e, xc_t[:, (bo + hh) * 128:(bo + hh + 1) * 128],
                       E["v_tok"][:, (c * KVH + h // 4) * 128:(c * KVH + h // 4 + 1) * 128],
                       E["dt_tok"][:, col:col + 1], Alu.mult)
                    ts(e, kd_t[:, (bo + hh) * 128:(bo + hh + 1) * 128],
                       E["k_tok"][:, (c * KVH + h // 4) * 128:(c * KVH + h // 4 + 1) * 128],
                       E["kdec_e"][:, col:col + 1], Alu.mult)
                e.drain().then_inc(S["pre"].h, 1)

            csv = S["cs"].take(1)

            @p.on("tensor")
            def _(e, gi=gi, prev=prev):
                e.wait_ge(S["pre"].h, prev)
                if gi >= 2:
                    e.wait_ge(S["up"].h, gi - 1)
                bank = 4 + (gi % 2)
                bo = (gi % 2) * 4
                last = None
                for hh in range(4):
                    last = e.matmul(PS[bank][:, hh * 128:(hh + 1) * 128],
                                    kd_t[:, (bo + hh) * 128:(bo + hh + 1) * 128],
                                    xc_t[:, (bo + hh) * 128:(bo + hh + 1) * 128],
                                    start=True, stop=True)
                last.then_inc(S["cs"].h, 1)

            S["up"].take(1)

            @p.on("vector")
            def _(e, c=c, hgrp=hgrp, gi=gi, csv=csv):
                e.wait_ge(S["cs"].h, csv)
                bank = 4 + (gi % 2)
                for hh in range(4):
                    h = hgrp * 4 + hh
                    col = c * H + h
                    e.drain()
                    e.scalar_tensor_tensor(
                        E["state"][:, h * 128:(h + 1) * 128],
                        E["state"][:, h * 128:(h + 1) * 128],
                        E["cdall"][:, col:col + 1],
                        PS[bank][:, hh * 128:(hh + 1) * 128],
                        Alu.mult, Alu.add)
                e.drain().then_inc(S["up"].h, 1)



def _exchange(nc, E):
    PS = E["PS"]
    with (
        nc.sbuf_tensor("x_gsum", [1, 32], F32) as gsum,
        nc.sbuf_tensor("x_gs4", [1, 128], F32) as gs4,
        nc.sbuf_tensor("x_gsb", [128, 128], F32) as gsb,
        nc.sbuf_tensor("x_acc", [128, 128], F32) as acc,
        nc.sbuf_tensor("x_coef", [128, 128], F32) as coef,
        nc.sbuf_tensor("x_sg", [128, H * D], F32) as sg,
        nc.sbuf_tensor("x_snew", [128, H * D], F32) as snew,
    ):
        p = Plan(nc, "xchg")
        s_v = p.sem("v"); s_d = p.sem("d"); s_cc = p.sem("cc")
        s_t = p.sem("t"); s_s = p.sem("s")

        vv = s_v.take(1)

        @p.on("vector")
        def _(e):
            e.tensor_reduce(gsum[0:1, :],
                            E["glrow"][0:1, :].rearrange("p (c h) -> p h c", c=NCH),
                            AxX, Alu.add)
            e.memset(snew[:, :], 0.0)
            e.drain().then_inc(s_v.h, 1)

        _, d1 = s_d.take_dma()
        _, d2 = s_d.take_dma()

        @p.on("sync")
        def _(e):
            e.wait_ge(s_v.h, vv)
            e.dma_start(out=bass.AP(E["cc_in"], 0, [[D * H, 128], [1, D * H]]),
                        in_=E["state"][:, :]).then_inc(s_d.h, 16)
            e.wait_ge(s_d.h, d1)
            e.dma_start(out=bass.AP(E["cc_in"], H * D * D, [[0, 1], [1, 32]]),
                        in_=gsum[0:1, :]).then_inc(s_d.h, 16)

        ccv = s_cc.take(1)

        @p.on("gpsimd")
        def _(e):
            e.wait_ge(s_d.h, d2)
            e.collective_compute(
                "AllGather", Alu.bypass,
                replica_groups=[[0, 1, 2, 3], [4, 5, 6, 7]],
                ins=[E["cc_in"][:, :].opt()],
                outs=[E["cc_out"][:, :].opt()]).then_inc(s_cc.h, 1)

        d2b, d3 = s_d.take_dma()

        @p.on("sync")
        def _(e):
            e.wait_ge(s_cc.h, ccv)
            e.wait_ge(s_d.h, d2b)
            e.dma_start(out=gs4[:, :].rearrange("p (a b) -> p a b", a=4),
                        in_=bass.AP(E["cc_out"], H * D * D, [[0, 1], [NCC, 4], [1, 32]])
                        ).then_inc(s_d.h, 16)

        tv = s_t.take(1)

        @p.on("tensor")
        def _(e):
            e.wait_ge(s_d.h, d3)
            e.matmul(PS[4][:, 0:128], E["ones_s"][:, :], gs4[:, :], start=True,
                     stop=True).then_inc(s_t.h, 1)

        vv2 = s_v.take(1)

        @p.on("vector")
        def _(e):
            e.wait_ge(s_t.h, tv)
            e.drain()
            e.tensor_copy(gsb[:, :], PS[4][:, 0:128])
            e.tensor_copy(acc[:, :], E["btile_s"][:, :])
            e.drain()
            for r in range(4):
                for rp in range(4):
                    e.scalar_tensor_tensor(acc[:, r * 32:(r + 1) * 32],
                                           gsb[:, rp * 32:(rp + 1) * 32],
                                           E["ecols_s"][:, r * 4 + rp:r * 4 + rp + 1],
                                           acc[:, r * 32:(r + 1) * 32],
                                           Alu.mult, Alu.add)
                    e.drain()
            e.drain().then_inc(s_v.h, 1)

        sv = s_s.take(1)

        @p.on("scalar")
        def _(e):
            e.wait_ge(s_v.h, vv2)
            e.activation(coef[:, :], acc[:, :], Act.Exp)
            e.drain().then_inc(s_s.h, 1)

        for r in range(4):
            combine_before = s_v.v
            dvp, dvr = s_d.take_dma()

            @p.on("sync")
            def _(e, r=r, combine_before=combine_before, dvp=dvp):
                if r > 0:
                    e.wait_ge(s_v.h, combine_before)
                e.wait_ge(s_d.h, dvp)
                e.dma_start(out=sg[:, :],
                            in_=bass.AP(E["cc_out"], r * NCC, [[D * H, 128], [1, D * H]])
                            ).then_inc(s_d.h, 16)

            s_v.take(1)

            @p.on("vector")
            def _(e, r=r, dvr=dvr):
                e.wait_ge(s_d.h, dvr)
                if r == 0:
                    e.wait_ge(s_s.h, sv)
                for h in range(H):
                    e.drain()
                    e.scalar_tensor_tensor(snew[:, h * 128:(h + 1) * 128],
                                           sg[:, h * 128:(h + 1) * 128],
                                           coef[:, r * 32 + h:r * 32 + h + 1],
                                           snew[:, h * 128:(h + 1) * 128],
                                           Alu.mult, Alu.add)
                e.drain().then_inc(s_v.h, 1)

        @p.on("vector")
        def _(e):
            e.drain()
            e.tensor_copy(E["state"][:, :], snew[:, :])
            e.drain()

        p.run()


def _p2_half(nc, E, half):
    with contextlib.ExitStack() as _st:
        xfm = _st.enter_context(nc.sbuf_tensor(f"p2_xfm{half}", [128, 32 * 512], BF16))
        wt = _st.enter_context(nc.sbuf_tensor(f"p2_wt{half}", [128, 2 * 8 * 512], BF16))
        q_st = _st.enter_context(nc.sbuf_tensor(f"p2_qst{half}", [128, 4 * 8 * 128], BF16))
        g_st = _st.enter_context(nc.sbuf_tensor(f"p2_gst{half}", [128, 4 * 8 * 128], BF16))
        qfm = _st.enter_context(nc.sbuf_tensor(f"p2_qfm{half}", [128, 4 * 8 * 128], BF16))
        kfm = _st.enter_context(nc.sbuf_tensor(f"p2_kfm{half}", [128, 4 * 2 * 128], BF16))
        yr_st = _st.enter_context(nc.sbuf_tensor(f"p2_yr{half}", [128, 4 * 8 * 128], BF16))
        gy_fm = _st.enter_context(nc.sbuf_tensor(f"p2_gy{half}", [128, 32 * 4 * 128], BF16))
        rtmp = _st.enter_context(nc.sbuf_tensor(f"p2_rt{half}", [128, 256], F32))
        dec = _st.enter_context(nc.sbuf_tensor(f"p2_dec{half}", [128, 2 * 128], F32))
        expg = _st.enter_context(nc.sbuf_tensor(f"p2_expg{half}", [128, 2 * 128], F32))
        sd = _st.enter_context(nc.sbuf_tensor(f"p2_sd{half}", [128, 2 * 128], BF16))
        qg = _st.enter_context(nc.sbuf_tensor(f"p2_qg{half}", [128, 2 * 128], BF16))
        stbf = _st.enter_context(nc.sbuf_tensor(f"p2_stbf{half}", [128, 2 * 128], BF16))
        xckd = _st.enter_context(nc.sbuf_tensor(f"p2_xckd{half}", [128, 2 * 256], BF16))
        growr = _st.enter_context(nc.sbuf_tensor(f"p2_grow{half}", [1, 2 * 128], F32))
        trash = _st.enter_context(nc.sbuf_tensor(f"p2_tr{half}", [128, 512], F32))
        ms = _st.enter_context(nc.sbuf_tensor(f"p2_ms{half}", [128, 8], F32))
        sgbuf = _st.enter_context(nc.sbuf_tensor(f"p2_sg{half}", [128, 128], BF16))
        gytmp = _st.enter_context(nc.sbuf_tensor(f"p2_gyt{half}", [128, 2 * 128], BF16))
        xb = _st.enter_context(nc.sbuf_tensor(f"p2_xb{half}", [128, 2 * 512], F32))
        r1b = _st.enter_context(nc.sbuf_tensor(f"p2_r1b{half}", [128, 2 * 512], F32))
        pcol = _st.enter_context(nc.sbuf_tensor(f"p2_pcol{half}", [128, 4], F32))
        _dummy = None

        p = Plan(nc, f"p2h{half}")
        S = dict(x=p.sem("x"), w=[p.sem("w0"), p.sem("w1")], mm=p.sem("mm"),
                 ev=p.sem("ev"), tp=p.sem("tp"), pea=p.sem("pea"), va=p.sem("va"),
                 sa=p.sem("sa"), vb=p.sem("vb"), peb=p.sem("peb"), sb=p.sem("sb"),
                 vc=p.sem("vc"), sil=p.sem("sil"), gyd=p.sem("gyd"),
                 xd=p.sem("xd"), r1d=p.sem("r1d"), pre=p.sem("pre"),
                 wev=p.sem("wev"))
        bufs = dict(xfm=xfm, wt=wt, q_st=q_st, g_st=g_st, qfm=qfm, kfm=kfm,
                    yr_st=yr_st, gy_fm=gy_fm, rtmp=rtmp, dec=dec, expg=expg,
                    sd=sd, qg=qg, stbf=stbf, xckd=xckd, growr=growr,
                    trash=trash, ms=ms, sgbuf=sgbuf, gytmp=gytmp, xb=xb,
                    r1b=r1b, pcol=pcol)

        xv = S["x"].take(16)

        @p.on("gpsimd")
        def _(e):
            ap_in = bass.AP(E["xT"], half * 512, [[N_TOK, 128], [N_TOK * 128, 32], [1, 512]])
            e.dma_start(out=xfm[:, :].rearrange("p (a b) -> p a b", a=32),
                        in_=ap_in).then_inc(S["x"].h, 16)

        it_ref = [0]
        for hg in range(4):
            _p2_hg(p, E, S, bufs, half, hg, xv, it_ref)
        _p2_wo(p, E, S, bufs, half)
        r1_fin, xd_fin = S["r1d"].v, S["xd"].v

        @p.on("sync")
        def _(e):
            e.wait_ge(S["r1d"].h, r1_fin)
            e.wait_ge(S["xd"].h, xd_fin)
        p.run()


def _p2_hg(p, E, S, bufs, half, hg, xv, it_ref):
    PS = E["PS"]
    xfm, wt = bufs["xfm"], bufs["wt"]
    q_st, g_st = bufs["q_st"], bufs["g_st"]

    for osu in range(4):
        is_q = osu < 2
        wdn = "WqT" if is_q else "WgT"
        c0 = hg * 1024 + (osu % 2) * 512
        ev_before = S["ev"].v
        gidx0 = hg * 16 + osu * 4
        for ig in range(4):
            gidx = gidx0 + ig
            par = gidx % 2
            wprev, wv = S["w"][par].take_dma()

            @p.on("gpsimd")
            def _(e, ig=ig, gidx=gidx, par=par, wprev=wprev, wdn=wdn, c0=c0):
                if gidx >= 2:
                    e.wait_ge(S["mm"].h, gidx - 1)
                if wprev:
                    e.wait_ge(S["w"][par].h, wprev)
                ap_w = bass.AP(E[wdn], (ig * 1024) * (H * D) + c0,
                               [[H * D, 128], [H * D * 128, 8], [1, 512]])
                buf = par * 8
                dst = wt[:, buf * 512:(buf + 8) * 512].rearrange("p (a b) -> p a b", a=8)
                e.dma_start(out=dst, in_=ap_w).then_inc(S["w"][par].h, 16)

            S["mm"].take(1)

            @p.on("tensor")
            def _(e, ig=ig, gidx=gidx, par=par, wv=wv, ev_before=ev_before):
                if ig == 0:
                    e.wait_ge(S["x"].h, xv)
                    e.wait_ge(S["ev"].h, ev_before)
                e.wait_ge(S["w"][par].h, wv)
                buf = par * 8
                last = None
                for irel in range(8):
                    i = ig * 8 + irel
                    for cc in range(4):
                        last = e.matmul(PS[cc][:, 0:512],
                                        xfm[:, i * 512 + cc * 128: i * 512 + cc * 128 + 128],
                                        wt[:, (buf + irel) * 512:(buf + irel + 1) * 512],
                                        start=(i == 0), stop=(i == 31))
                last.then_inc(S["mm"].h, 1)

        mm_done = S["mm"].v
        S["ev"].take(1)

        if is_q:
            @p.on("vector")
            def _(e, osu=osu, hg=hg, mm_done=mm_done):
                e.wait_ge(S["mm"].h, mm_done)
                rtmp = bufs["rtmp"]
                for cc in range(4):
                    c = half * 4 + cc
                    for hr in range(4):
                        hh = (osu % 2) * 4 + hr
                        psl = PS[cc][:, hr * 128:(hr + 1) * 128]
                        cosl = E["cos_s"][:, c * 128:(c + 1) * 128]
                        e.drain()
                        e.tensor_tensor(rtmp[:, 0:128], psl, cosl, Alu.mult)
                        e.tensor_tensor(rtmp[:, 128:192],
                                        PS[cc][:, hr * 128 + 64:hr * 128 + 128],
                                        E["sinS_s"][:, c * 128:c * 128 + 64], Alu.mult)
                        e.tensor_tensor(rtmp[:, 192:256], PS[cc][:, hr * 128:hr * 128 + 64],
                                        E["sinS_s"][:, c * 128 + 64:c * 128 + 128], Alu.mult)
                        e.drain()
                        e.tensor_tensor(q_st[:, (cc * 8 + hh) * 128:(cc * 8 + hh + 1) * 128],
                                        rtmp[:, 0:128], rtmp[:, 128:256], Alu.add)
                e.drain().then_inc(S["ev"].h, 1)
        else:
            @p.on("scalar")
            def _(e, osu=osu, mm_done=mm_done):
                e.wait_ge(S["mm"].h, mm_done)
                for cc in range(4):
                    for hr in range(4):
                        hh = (osu % 2) * 4 + hr
                        e.copy(g_st[:, (cc * 8 + hh) * 128:(cc * 8 + hh + 1) * 128],
                               PS[cc][:, hr * 128:(hr + 1) * 128])
                e.drain().then_inc(S["ev"].h, 1)

    proj_done = S["ev"].v

    peb_prev = S["peb"].v
    tp_base = S["tp"].v
    tpv = S["tp"].take(16 * 40)

    @p.on("sync")
    def _(e, hg=hg, proj_done=proj_done, peb_prev=peb_prev, tp_base=tp_base):
        e.wait_ge(S["ev"].h, proj_done)
        if peb_prev:
            e.wait_ge(S["peb"].h, peb_prev)
        n = tp_base
        qfm, kfm = bufs["qfm"], bufs["kfm"]
        for cc in range(4):
            c = half * 4 + cc
            for hh in range(8):
                if n:
                    e.wait_ge(S["tp"].h, n)
                e.dma_start(out=qfm[:, (cc * 8 + hh) * 128:(cc * 8 + hh + 1) * 128],
                            in_=q_st[:, (cc * 8 + hh) * 128:(cc * 8 + hh + 1) * 128],
                            transpose=True).then_inc(S["tp"].h, 16)
                n += 16
            for kk in range(2):
                kvh = hg * 2 + kk
                if n:
                    e.wait_ge(S["tp"].h, n)
                e.dma_start(out=kfm[:, (cc * 2 + kk) * 128:(cc * 2 + kk + 1) * 128],
                            in_=E["k_tok"][:, (c * KVH + kvh) * 128:(c * KVH + kvh + 1) * 128],
                            transpose=True).then_inc(S["tp"].h, 16)
                n += 16

    for hh in range(8):
        h = hg * 8 + hh
        for cc in range(4):
            _p2_core_iter(p, E, S, bufs, half, hg, hh, h, cc, tpv,
                          first_of_hg=(hh == 0 and cc == 0), it_ref=it_ref)

    core_done_vc = S["vc"].v
    for hh in range(8):
        h = hg * 8 + hh
        for cc in range(4):
            j = hh * 4 + cc
            prev_vgy = S["vc"].v
            silv = S["sil"].take(1)

            @p.on("scalar")
            def _(e, hh=hh, cc=cc, j=j, core_done_vc=core_done_vc, prev_vgy=prev_vgy):
                if j == 0:
                    e.wait_ge(S["vc"].h, core_done_vc)
                else:
                    e.wait_ge(S["vc"].h, prev_vgy)
                e.activation(bufs["sgbuf"][:, :],
                             g_st[:, (cc * 8 + hh) * 128:(cc * 8 + hh + 1) * 128],
                             Act.Silu)
                e.drain().then_inc(S["sil"].h, 1)

            gy_jm2 = S["gyd"].v - 16 if S["gyd"].v >= 32 else 0
            vgy = S["vc"].take(1)

            @p.on("vector")
            def _(e, hh=hh, cc=cc, j=j, silv=silv, gy_jm2=gy_jm2):
                e.wait_ge(S["sil"].h, silv)
                if gy_jm2:
                    e.wait_ge(S["gyd"].h, gy_jm2)
                e.drain()
                e.tensor_tensor(bufs["gytmp"][:, (j % 2) * 128:(j % 2) * 128 + 128],
                                bufs["yr_st"][:, (cc * 8 + hh) * 128:(cc * 8 + hh + 1) * 128],
                                bufs["sgbuf"][:, :], Alu.mult)
                e.drain().then_inc(S["vc"].h, 1)

            gp, gv = S["gyd"].take_dma()

            @p.on("sync")
            def _(e, hh=hh, cc=cc, h=h, j=j, vgy=vgy, gp=gp):
                e.wait_ge(S["vc"].h, vgy)
                if gp:
                    e.wait_ge(S["gyd"].h, gp)
                e.dma_start(out=bufs["gy_fm"][:, (h * 4 + cc) * 128:(h * 4 + cc + 1) * 128],
                            in_=bufs["gytmp"][:, (j % 2) * 128:(j % 2) * 128 + 128],
                            transpose=True).then_inc(S["gyd"].h, 16)


def _p2_core_iter(p, E, S, bufs, half, hg, hh, h, cc, tpv, first_of_hg, it_ref):
    PS = E["PS"]
    it = it_ref[0]
    it_ref[0] += 1
    C = half * 4 + cc
    col = C * H + h
    par = it % 2
    bA, bB = 4 + 2 * par, 5 + 2 * par
    qsl = bufs["qfm"][:, (cc * 8 + hh) * 128:(cc * 8 + hh + 1) * 128]
    ksl = bufs["kfm"][:, (cc * 2 + (hh // 4)) * 128:(cc * 2 + (hh // 4) + 1) * 128]
    dec = bufs["dec"][:, par * 128:(par + 1) * 128]
    expg = bufs["expg"][:, par * 128:(par + 1) * 128]
    sd = bufs["sd"][:, par * 128:(par + 1) * 128]
    qg = bufs["qg"][:, par * 128:(par + 1) * 128]
    stbf = bufs["stbf"][:, par * 128:(par + 1) * 128]
    xc = bufs["xckd"][:, par * 256:par * 256 + 128]
    kd = bufs["xckd"][:, par * 256 + 128:par * 256 + 256]
    growr = bufs["growr"][0:1, par * 128:(par + 1) * 128]
    msc = bufs["ms"][:, par * 4:(par + 1) * 4]
    trash = bufs["trash"][:, 0:128]

    prev = S["pre"].take(1)
    vc_bank_free = max(0, S["vc"].v - 1)

    @p.on("vector")
    def _(e, col=col, C=C, h=h):
        if it >= 2:
            e.wait_ge(S["peb"].h, it - 1)
        e.drain()
        ts(e, xc, E["v_tok"][:, (C * KVH + h // 4) * 128:(C * KVH + h // 4 + 1) * 128],
           E["dt_tok"][:, col:col + 1], Alu.mult)
        ts(e, kd, E["k_tok"][:, (C * KVH + h // 4) * 128:(C * KVH + h // 4 + 1) * 128],
           E["kdec_e"][:, col:col + 1], Alu.mult)
        e.tensor_copy(stbf, E["state"][:, h * 128:(h + 1) * 128])
        e.drain().then_inc(S["pre"].h, 1)

    pea = S["pea"].take(1)

    @p.on("tensor")
    def _(e, col=col, prev=prev, vcb=vc_bank_free, it=it):
        if first_of_hg:
            e.wait_ge(S["tp"].h, tpv)
        e.wait_ge(S["pre"].h, prev)
        if vcb:
            e.wait_ge(S["vc"].h, vcb)
        e.transpose(PS[bA][0:1, 128:256], E["G_tok"][:, col:col + 1],
                    E["ident_s"][:, :]).then_inc(S["pea"].h, 1)

    vga = S["va"].take(1)

    @p.on("vector")
    def _(e, pea=pea):
        e.wait_ge(S["pea"].h, pea)
        e.drain()
        e.tensor_copy(growr, PS[bA][0:1, 128:256])
        e.drain().then_inc(S["va"].h, 1)

    pea2 = S["pea"].take(1)

    @p.on("tensor")
    def _(e, vga=vga):
        e.wait_ge(S["va"].h, vga)
        e.matmul(PS[bA][:, 0:128], E["ones_s"][:, :], growr, start=True, stop=True)
        e.matmul(PS[bB][:, 0:128], ksl, qsl, start=True,
                 stop=True).then_inc(S["pea"].h, 1)

    vaa = S["va"].take(1)

    @p.on("vector")
    def _(e, pea2=pea2, col=col):
        e.wait_ge(S["pea"].h, pea2)
        e.drain()
        e.scalar_tensor_tensor(dec, PS[bA][:, 0:128], E["G_tok"][:, col:col + 1],
                               E["mask_s"][:, :], Alu.subtract, Alu.add)
        e.drain().then_inc(S["va"].h, 1)

    saa = S["sa"].take(1)

    @p.on("scalar")
    def _(e, vaa=vaa):
        e.wait_ge(S["va"].h, vaa)
        e.activation(dec, dec, Act.Exp)
        e.activation(expg, PS[bA][:, 0:128], Act.Exp)
        e.drain().then_inc(S["sa"].h, 1)

    vbb = S["vb"].take(1)

    @p.on("vector")
    def _(e, saa=saa):
        e.wait_ge(S["sa"].h, saa)
        e.drain()
        e.tensor_tensor(sd, PS[bB][:, 0:128], dec, Alu.mult)
        e.tensor_tensor(qg, qsl, expg, Alu.mult)
        e.drain().then_inc(S["vb"].h, 1)

    peb = S["peb"].take(1)

    @p.on("tensor")
    def _(e, vbb=vbb):
        e.wait_ge(S["vb"].h, vbb)
        e.matmul(PS[bA][:, 256:384], sd, xc, start=True, stop=False)
        e.matmul(PS[bA][:, 256:384], qg, stbf, start=False, stop=True)
        e.matmul(PS[bB][:, 256:384], kd, xc, start=True,
                 stop=True).then_inc(S["peb"].h, 1)

    sbb = S["sb"].take(1)

    @p.on("scalar")
    def _(e, peb=peb):
        e.wait_ge(S["peb"].h, peb)
        e.activation(trash, PS[bA][:, 256:384], Act.Square, accum_out=msc[:, 0:1])
        e.drain()
        e.activation(msc[:, 1:2], msc[:, 0:1], Act.Ln, scale=E["actc"][:, 2:3],
                     bias=E["actc"][:, 0:1])
        e.drain()
        e.activation(msc[:, 2:3], msc[:, 1:2], Act.Exp, scale=-0.5)
        e.drain().then_inc(S["sb"].h, 1)

    S["vc"].take(1)

    @p.on("vector")
    def _(e, sbb=sbb, col=col, h=h, hh=hh, cc=cc):
        e.wait_ge(S["sb"].h, sbb)
        e.drain()
        ts(e, bufs["rtmp"][:, 0:128], PS[bA][:, 256:384], msc[:, 2:3], Alu.mult)
        e.drain()
        e.tensor_tensor(bufs["yr_st"][:, (cc * 8 + hh) * 128:(cc * 8 + hh + 1) * 128],
                        bufs["rtmp"][:, 0:128], E["gnorm_s"][:, :], Alu.mult)
        e.scalar_tensor_tensor(E["state"][:, h * 128:(h + 1) * 128],
                               E["state"][:, h * 128:(h + 1) * 128],
                               E["cdall"][:, col:col + 1],
                               PS[bB][:, 256:384], Alu.mult, Alu.add)
        e.drain().then_inc(S["vc"].h, 1)


def _p2_wo(p, E, S, bufs, half):
    PS = E["PS"]
    gy_fm, wt = bufs["gy_fm"], bufs["wt"]
    xb, r1b, trash, pcol = bufs["xb"], bufs["r1b"], bufs["trash"], bufs["pcol"]
    gy_done = S["gyd"].v
    mm0 = S["mm"].v

    for osu in range(8):
        o0 = osu * 512
        wo_ev_before = S["wev"].v
        gidx0 = 64 + osu * 4
        for ig in range(4):
            gidx = gidx0 + ig
            par = gidx % 2
            wprev, wv = S["w"][par].take_dma()

            @p.on("gpsimd")
            def _(e, ig=ig, gidx=gidx, par=par, wprev=wprev, o0=o0):
                e.wait_ge(S["mm"].h, gidx - 1)
                if wprev:
                    e.wait_ge(S["w"][par].h, wprev)
                ap_w = bass.AP(E["WoT"], (ig * 1024) * HID + o0,
                               [[HID, 128], [HID * 128, 8], [1, 512]])
                buf = par * 8
                dst = wt[:, buf * 512:(buf + 8) * 512].rearrange("p (a b) -> p a b", a=8)
                e.dma_start(out=dst, in_=ap_w).then_inc(S["w"][par].h, 16)

            S["mm"].take(1)

            @p.on("tensor")
            def _(e, ig=ig, gidx=gidx, par=par, wv=wv, wo_ev_before=wo_ev_before,
                  gy_done=gy_done):
                if ig == 0:
                    e.wait_ge(S["gyd"].h, gy_done)
                    e.wait_ge(S["wev"].h, wo_ev_before)
                e.wait_ge(S["w"][par].h, wv)
                buf = par * 8
                last = None
                for hrel in range(8):
                    hd = ig * 8 + hrel
                    for cc in range(4):
                        last = e.matmul(
                            PS[cc][:, 0:512],
                            gy_fm[:, (hd * 4 + cc) * 128:(hd * 4 + cc + 1) * 128],
                            wt[:, (buf + hrel) * 512:(buf + hrel + 1) * 512],
                            start=(hd == 0), stop=(hd == 31))
                last.then_inc(S["mm"].h, 1)

        mm_done = S["mm"].v

        for cc in range(4):
            C = half * 4 + cc
            xp, xv2 = S["xd"].take_dma()

            vc_now = S["vc"].v

            @p.on("sync")
            def _(e, cc=cc, C=C, osu=osu, xp=xp, vc_now=vc_now):
                if xp:
                    e.wait_ge(S["xd"].h, xp)
                if osu or cc:
                    e.wait_ge(S["vc"].h, vc_now)
                e.dma_start(out=xb[:, (cc % 2) * 512:(cc % 2) * 512 + 512],
                            in_=bass.AP(E["x_tok"], (C * 128) * HID + osu * 512,
                                        [[HID, 128], [1, 512]])).then_inc(S["xd"].h, 16)

            r1_now = S["r1d"].v
            wvv = S["vc"].take(1)

            @p.on("vector")
            def _(e, cc=cc, mm_done=mm_done, xv2=xv2, r1_now=r1_now):
                e.wait_ge(S["mm"].h, mm_done)
                e.wait_ge(S["xd"].h, xv2)
                if r1_now >= 32:
                    e.wait_ge(S["r1d"].h, r1_now - 16)
                e.drain()
                e.tensor_tensor(r1b[:, (cc % 2) * 512:(cc % 2) * 512 + 512],
                                PS[cc][:, 0:512],
                                xb[:, (cc % 2) * 512:(cc % 2) * 512 + 512], Alu.add)
                e.drain().then_inc(S["vc"].h, 1)

            sbv = S["sb"].take(1)

            @p.on("scalar")
            def _(e, cc=cc, wvv=wvv):
                e.wait_ge(S["vc"].h, wvv)
                sl = r1b[:, (cc % 2) * 512:(cc % 2) * 512 + 512]
                e.activation(trash[:, 0:512], sl, Act.Square, accum_out=pcol[:, 0:1])
                e.drain()
                e.activation(trash[:, 0:512], sl, Act.Identity, accum_out=pcol[:, 1:2])
                e.drain().then_inc(S["sb"].h, 1)

            wev2 = S["wev"].take(1)

            @p.on("vector")
            def _(e, cc=cc, C=C, sbv=sbv):
                e.wait_ge(S["sb"].h, sbv)
                e.drain()
                e.tensor_tensor(E["st_sq"][:, C:C + 1], E["st_sq"][:, C:C + 1],
                                pcol[:, 0:1], Alu.add)
                e.tensor_tensor(E["st_sum"][:, C:C + 1], E["st_sum"][:, C:C + 1],
                                pcol[:, 1:2], Alu.add)
                e.drain().then_inc(S["wev"].h, 1)

            r1p, r1v = S["r1d"].take_dma()

            @p.on("sync")
            def _(e, cc=cc, C=C, osu=osu, wvv=wvv, r1p=r1p):
                e.wait_ge(S["vc"].h, wvv)
                if r1p:
                    e.wait_ge(S["r1d"].h, r1p)
                e.dma_start(out=bass.AP(E["r1_scr"], (C * 128) * HID + osu * 512,
                                        [[HID, 128], [1, 512]]),
                            in_=r1b[:, (cc % 2) * 512:(cc % 2) * 512 + 512]
                            ).then_inc(S["r1d"].h, 16)


def _p25_ln1(nc, E):
    PS = E["PS"]
    with (
        nc.sbuf_tensor("p25_r1", [128, HID], F32) as r1buf,
        nc.sbuf_tensor("p25_x1", [128, HID], F32) as x1,
        nc.sbuf_tensor("p25_xq", [128, HID], BF16) as xq,
        nc.sbuf_tensor("p25_am", [128, 32], F32) as am,
        nc.sbuf_tensor("p25_sc", [128, 32], F32) as sc,
        nc.sbuf_tensor("p25_rc", [128, 32], F32) as rc,
        nc.sbuf_tensor("p25_u", [128, 128], F32) as ubuf,
        nc.sbuf_tensor("p25_mv", [128, 8], F32) as mv,
    ):
        p = Plan(nc, "p25")
        s_d = p.sem("d"); s_v = p.sem("v"); s_s = p.sem("s"); s_t = p.sem("t")
        for c in range(NCH):
            dp, dv = s_d.take_dma()

            @p.on("sync")
            def _(e, c=c, dp=dp):
                if dp:
                    e.wait_ge(s_d.h, dp)
                if c >= 1:
                    e.wait_ge(s_t.h, 16 * 32 * c)
                e.dma_start(out=r1buf[:, :],
                            in_=bass.AP(E["r1_scr"], c * 128 * HID, [[HID, 128], [1, HID]])
                            ).then_inc(s_d.h, 16)

            sv = s_s.take(1)

            @p.on("scalar")
            def _(e, c=c, dv=dv):
                e.wait_ge(s_d.h, dv)
                e.activation(mv[:, 0:1], E["st_sum"][:, c:c + 1], Act.Identity,
                             scale=E["actc"][:, 3:4])
                e.drain()
                e.activation(mv[:, 1:2], mv[:, 0:1], Act.Square)
                e.drain()
                e.activation(mv[:, 2:3], E["st_sq"][:, c:c + 1], Act.Identity,
                             scale=E["actc"][:, 3:4])
                e.drain().then_inc(s_s.h, 1)

            vv = s_v.take(1)

            @p.on("vector")
            def _(e, c=c, sv=sv):
                e.wait_ge(s_s.h, sv)
                e.drain()
                e.tensor_sub(mv[:, 3:4], mv[:, 2:3], mv[:, 1:2])
                e.drain().then_inc(s_v.h, 1)

            sv2 = s_s.take(1)

            @p.on("scalar")
            def _(e, c=c, vv=vv):
                e.wait_ge(s_v.h, vv)
                e.activation(mv[:, 4:5], mv[:, 3:4], Act.Ln, bias=E["actc"][:, 1:2])
                e.drain()
                e.activation(mv[:, 5:6], mv[:, 4:5], Act.Exp, scale=-0.5)
                e.drain()
                e.copy(E["ln1_mean"][:, c:c + 1], mv[:, 0:1])
                e.copy(E["ln1_inv"][:, c:c + 1], mv[:, 5:6])
                e.drain().then_inc(s_s.h, 1)

            vv2 = s_v.take(1)

            @p.on("vector")
            def _(e, c=c, sv2=sv2):
                e.wait_ge(s_s.h, sv2)
                if c >= 1:
                    e.wait_ge(s_t.h, 16 * 32 * c)
                e.drain()
                ts(e, x1[:, :], r1buf[:, :], mv[:, 0:1], Alu.subtract, mv[:, 5:6], Alu.mult)
                e.drain()
                e.tensor_reduce(am[:, :], x1[:, :].rearrange("p (g k) -> p g k", g=32),
                                AxX, Alu.max, apply_absolute_value=True)
                e.drain()
                ts(e, sc[:, :], am[:, :], 1.0 / 127.0, Alu.mult, QEPS, Alu.max)
                e.drain()
                e.reciprocal(rc[:, :], sc[:, :])
                e.drain()
                for g in range(32):
                    ts(e, ubuf[:, :], x1[:, g * 128:(g + 1) * 128], rc[:, g:g + 1],
                       Alu.mult, MAGIC, Alu.add)
                    e.drain()
                    ts(e, xq[:, g * 128:(g + 1) * 128], ubuf[:, :], MAGIC,
                       Alu.subtract, sc[:, g:g + 1], Alu.mult)
                    e.drain()
                e.drain().then_inc(s_v.h, 1)

            for g in range(32):
                tp, tv = s_t.take_dma()

                @p.on("sync")
                def _(e, c=c, g=g, vv2=vv2, tp=tp):
                    if g == 0:
                        e.wait_ge(s_v.h, vv2)
                    if tp:
                        e.wait_ge(s_t.h, tp)
                    e.dma_start(out=E["xq_fm"][:, (g * 8 + c) * 128:(g * 8 + c + 1) * 128],
                                in_=xq[:, g * 128:(g + 1) * 128],
                                transpose=True).then_inc(s_t.h, 16)
        t_fin, d_fin = s_t.v, s_d.v

        @p.on("sync")
        def _(e):
            e.wait_ge(s_t.h, t_fin)
            e.wait_ge(s_d.h, d_fin)
        p.run()


def _quant_tile(e, E, land, out_bf, am, sc, rc, ngroups, ubuf):
    e.drain()
    e.tensor_reduce(am[:, 0:ngroups],
                    land.rearrange("p (g k) -> p g k", k=128),
                    AxX, Alu.max, apply_absolute_value=True)
    e.drain()
    ts(e, sc[:, 0:ngroups], am[:, 0:ngroups], 1.0 / 127.0, Alu.mult, QEPS, Alu.max)
    e.drain()
    e.reciprocal(rc[:, 0:ngroups], sc[:, 0:ngroups])
    e.drain()
    for g in range(ngroups):
        par = g % 2
        if g >= 2:
            e.drain()
        ts(e, ubuf[:, par * 128:(par + 1) * 128], land[:, g * 128:(g + 1) * 128],
           rc[:, g:g + 1], Alu.mult, MAGIC, Alu.add)
        e.drain()
        ts(e, out_bf[:, g * 128:(g + 1) * 128], ubuf[:, par * 128:(par + 1) * 128],
           MAGIC, Alu.subtract, sc[:, g:g + 1], Alu.mult)
    e.drain()


def _p3_ffn_gu(nc, E):
    PS = E["PS"]
    with contextlib.ExitStack() as _st:
        wland = _st.enter_context(nc.sbuf_tensor("p3_wland", [128, 4 * 2048], F32))
        wqb = _st.enter_context(nc.sbuf_tensor("p3_wqb", [128, 4 * 2048], BF16))
        wtb = _st.enter_context(nc.sbuf_tensor("p3_wtb", [128, 32 * 512], BF16))
        sil = _st.enter_context(nc.sbuf_tensor("p3_sil", [128, 8 * 512], F32))
        hid = _st.enter_context(nc.sbuf_tensor("p3_hid", [128, 2 * 512], F32))
        hqb = _st.enter_context(nc.sbuf_tensor("p3_hqb", [128, 2 * 512], BF16))
        am = _st.enter_context(nc.sbuf_tensor("p3_am", [128, 16], F32))
        scb = _st.enter_context(nc.sbuf_tensor("p3_scb", [128, 4 * 16], F32))
        rcb = _st.enter_context(nc.sbuf_tensor("p3_rcb", [128, 4 * 16], F32))
        ham = _st.enter_context(nc.sbuf_tensor("p3_ham", [128, 12], F32))
        dland = _st.enter_context(nc.sbuf_tensor("p3_dland", [128, 2 * 1408], F32))
        dqb = _st.enter_context(nc.sbuf_tensor("p3_dqb", [128, 2 * 1408], BF16))
        dam = _st.enter_context(nc.sbuf_tensor("p3_dam", [128, 2 * 33], F32))
        qub = _st.enter_context(nc.sbuf_tensor("p3_qub", [128, 256], F32))
        qup = _st.enter_context(nc.sbuf_tensor("p3_qup", [128, 512], F32))

        E["p3_qub"] = qub
        p = Plan(nc, "p3")
        S = dict(wl=[p.sem(f"wl{i}") for i in range(4)], qsc=p.sem("qsc"),
                 qav=p.sem("qav"), qap=p.sem("qap"),
                 qs=p.sem("qs"), trd=p.sem("trd"), mm=p.sem("mm"),
                 sil=p.sem("sil"), hv=p.sem("hv"), hqd=p.sem("hqd"),
                 dl=[p.sem("dl0"), p.sem("dl1")], dsc=p.sem("dsc"),
                 dav=p.sem("dav"), dap=p.sem("dap"), dqs=p.sem("dqs"))
        S["hcnt"] = [0]
        S["slot_lastap"] = [None] * 4
        S["slot_lastst"] = [0] * 4
        S["ucnt"] = [0]
        S["dslot_lastap"] = [None] * 2
        S["dslot_lastst"] = [0] * 2

        wd_qwidths = [1408, 1408, 1408, 1408, 1408, 1408, 1280, 1280]
        wd_qoffs = [0, 1408, 2816, 4224, 5632, 7040, 8448, 9728]
        wd_units = [(ot, qt) for ot in range(32) for qt in range(8)]
        wd_i = [0]

        for og in range(22):
            width = 512 if og < 21 else 256
            ntile = 4 if og < 21 else 2
            _p3_quant_mat(p, E, S, wland, wqb, am, scb, rcb, qub, qup,
                          og, ntile, "Wg_f", "wgq_scr")
            _p3_mm_mat(p, E, S, wtb, sil, hid, hqb, ham, qub,
                       og, width, "wgq_scr", "g")
            _p3_quant_mat(p, E, S, wland, wqb, am, scb, rcb, qub, qup,
                          og, ntile, "Wu_f", "wuq_scr")
            nwd = 12 if og < 21 else (len(wd_units) - wd_i[0])
            for _ in range(nwd):
                if wd_i[0] >= len(wd_units):
                    break
                ot, qt = wd_units[wd_i[0]]
                wd_i[0] += 1
                _p3_wd_unit(p, E, S, dland, dqb, dam, qub, qup, ot, qt,
                            wd_qoffs[qt], wd_qwidths[qt])
            _p3_mm_mat(p, E, S, wtb, sil, hid, hqb, ham, qub,
                       og, width, "wuq_scr", "u")

        fin = [S["hqd"].v, S["dqs"].v, S["qs"].v]

        @p.on("sync")
        def _(e):
            e.wait_ge(S["hqd"].h, fin[0])
            e.wait_ge(S["dqs"].h, fin[1])
            e.wait_ge(S["qs"].h, fin[2])
        p.run()


def _p3_quant_mat(p, E, S, wland, wqb, am, scb, rcb, qub, qup, og, ntile,
                  wdram, wscr):
    qs_base = S["qs"].v
    halves = []
    for i, (ot, half) in enumerate([(o, hf) for o in range(ntile) for hf in range(2)]):
        h = S["hcnt"][0]
        S["hcnt"][0] += 1
        slot = h % 4
        eng = "vector"
        qsem = S["qav"]
        qv = qsem.take(1)
        lprev, lv = S["wl"][slot].take_dma()
        lastap = S["slot_lastap"][slot]
        laststv = S["slot_lastst"][slot]
        S["slot_lastap"][slot] = (qsem, qv)
        S["slot_lastst"][slot] = qs_base + 16 * (i + 1)
        halves.append((h, ot, half, slot, eng, qsem, qv, lv, lastap, laststv))

    for h, ot, half, slot, eng, qsem, qv, lv, lastap, laststv in halves:

        @p.on("sync")
        def _(e, og=og, ot=ot, half=half, slot=slot, lv=lv,
              lastap=lastap, wdram=wdram):
            if lastap is not None:
                e.wait_ge(lastap[0].h, lastap[1])
            if lv > 16:
                e.wait_ge(S["wl"][slot].h, lv - 16)
            e.dma_start(out=wland[:, slot * 2048:(slot + 1) * 2048],
                        in_=bass.AP(E[wdram], (og * 512 + ot * 128) * HID + half * 2048,
                                    [[HID, 128], [1, 2048]])
                        ).then_inc(S["wl"][slot].h, 16)

    for h, ot, half, slot, eng, qsem, qv, lv, lastap, laststv in halves:
        qscv = S["qsc"].take(1)

        @p.on("vector")
        def _(e, slot=slot, lv=lv):
            e.wait_ge(S["wl"][slot].h, lv)
            e.drain()
            e.tensor_reduce(am[:, 0:16],
                            wland[:, slot * 2048:(slot + 1) * 2048]
                            .rearrange("p (g k) -> p g k", k=128),
                            AxX, Alu.max, apply_absolute_value=True)
            e.drain()
            ts(e, scb[:, slot * 16:slot * 16 + 16], am[:, 0:16],
               1.0 / 127.0, Alu.mult, QEPS, Alu.max)
            e.drain()
            e.reciprocal(rcb[:, slot * 16:slot * 16 + 16],
                         scb[:, slot * 16:slot * 16 + 16])
            e.drain().then_inc(S["qsc"].h, 1)

        ub = qub if eng == "vector" else qup

        @p.on(eng)
        def _(e, slot=slot, qscv=qscv, laststv=laststv, ub=ub, eng=eng,
              lv=lv, qsem=qsem):
            if eng == "gpsimd":
                e.wait_ge(S["qsc"].h, qscv)
                e.wait_ge(S["wl"][slot].h, lv)
            if laststv:
                e.wait_ge(S["qs"].h, laststv)
            e.drain()
            for g in range(16):
                par = g % 2
                if g >= 2:
                    e.drain()
                ts(e, ub[:, par * 128:(par + 1) * 128],
                   wland[:, slot * 2048 + g * 128:slot * 2048 + (g + 1) * 128],
                   rcb[:, slot * 16 + g:slot * 16 + g + 1], Alu.mult,
                   MAGIC, Alu.add)
                e.drain()
                ts(e, wqb[:, slot * 2048 + g * 128:slot * 2048 + (g + 1) * 128],
                   ub[:, par * 128:(par + 1) * 128],
                   MAGIC, Alu.subtract,
                   scb[:, slot * 16 + g:slot * 16 + g + 1], Alu.mult)
            e.drain().then_inc(qsem.h, 1)

    for h, ot, half, slot, eng, qsem, qv, lv, lastap, laststv in halves:
        qsp, qsv = S["qs"].take_dma()

        @p.on("sync")
        def _(e, og=og, ot=ot, half=half, slot=slot, qsem=qsem, qv=qv,
              qsp=qsp, wscr=wscr):
            e.wait_ge(qsem.h, qv)
            if qsp:
                e.wait_ge(S["qs"].h, qsp)
            e.dma_start(out=bass.AP(E[wscr], (og * 512 + ot * 128) * HID + half * 2048,
                                    [[HID, 128], [1, 2048]]),
                        in_=wqb[:, slot * 2048:(slot + 1) * 2048]
                        ).then_inc(S["qs"].h, 16)


def _p3_mm_mat(p, E, S, wtb, sil, hid, hqb, ham, qub, og, width, wscr, mat):
    PS = E["PS"]
    qs_done = S["qs"].v
    mm_before = S["mm"].v
    sil_before = S["sil"].v
    hv_before = S["hv"].v

    trd_base = S["trd"].v
    trd_fin = S["trd"].take(16 * 32)

    @p.on("scalar")
    def _(e, og=og, width=width, wscr=wscr, qs_done=qs_done, trd_base=trd_base,
          mm_before=mm_before):
        e.wait_ge(S["qs"].h, qs_done)
        if mm_before:
            e.wait_ge(S["mm"].h, mm_before)
        n = trd_base
        for ich in range(32):
            if n:
                e.wait_ge(S["trd"].h, n)
            e.dma_start(out=wtb[:, ich * 512:ich * 512 + width],
                        in_=bass.AP(E[wscr], (og * 512) * HID + ich * 128,
                                    [[HID, width], [1, 128]]),
                        transpose=True).then_inc(S["trd"].h, 16)
            n += 16

    mmv = S["mm"].take(1)

    @p.on("tensor")
    def _(e, og=og, width=width, trd_fin=trd_fin, sil_before=sil_before,
          hv_before=hv_before, mat=mat):
        e.wait_ge(S["trd"].h, trd_fin)
        if mat == "g":
            if hv_before:
                e.wait_ge(S["hv"].h, hv_before)
        else:
            e.wait_ge(S["sil"].h, sil_before)
        last = None
        for i in range(32):
            for c in range(8):
                last = e.matmul(PS[c][:, 0:width],
                                E["xq_fm"][:, (i * 8 + c) * 128:(i * 8 + c + 1) * 128],
                                wtb[:, i * 512:i * 512 + width],
                                start=(i == 0), stop=(i == 31))
        last.then_inc(S["mm"].h, 1)

    if mat == "g":
        hv_now = S["hv"].v
        S["sil"].take(1)

        @p.on("scalar")
        def _(e, og=og, width=width, mmv=mmv, hv_now=hv_now):
            e.wait_ge(S["mm"].h, mmv)
            if hv_now:
                e.wait_ge(S["hv"].h, hv_now)
            for c in range(8):
                e.activation(sil[:, c * 512:c * 512 + width], PS[c][:, 0:width], Act.Silu)
            e.drain().then_inc(S["sil"].h, 1)
    else:
        for c in range(8):
            par = c % 2
            hq_prev = S["hqd"].v - 16 if S["hqd"].v >= 32 else 0
            hvv = S["hv"].take(1)

            @p.on("vector")
            def _(e, c=c, og=og, width=width, par=par, mmv=mmv, hq_prev=hq_prev):
                if c == 0:
                    e.wait_ge(S["mm"].h, mmv)
                if hq_prev:
                    e.wait_ge(S["hqd"].h, hq_prev)
                e.drain()
                e.tensor_tensor(hid[:, par * 512:par * 512 + width],
                                sil[:, c * 512:c * 512 + width],
                                PS[c][:, 0:width], Alu.mult)
                _quant_tile(e, E, hid[:, par * 512:par * 512 + width],
                            hqb[:, par * 512:par * 512 + width],
                            ham[:, 0:4], ham[:, 4:8], ham[:, 8:12],
                            width // 128, qub)
                e.drain().then_inc(S["hv"].h, 1)

            hqp, hqv = S["hqd"].take_dma()

            @p.on("sync")
            def _(e, c=c, og=og, width=width, par=par, hvv=hvv, hqp=hqp):
                e.wait_ge(S["hv"].h, hvv)
                if hqp:
                    e.wait_ge(S["hqd"].h, hqp)
                e.dma_start(out=bass.AP(E["hq_scr"], (c * 128) * I_FFN + og * 512,
                                        [[I_FFN, 128], [1, width]]),
                            in_=hqb[:, par * 512:par * 512 + width]
                            ).then_inc(S["hqd"].h, 16)


def _p3_wd_unit(p, E, S, dland, dqb, dam, qub, qup, ot, qt, qoff, qwidth):
    u = S["ucnt"][0]
    S["ucnt"][0] += 1
    slot = u % 2
    ng = qwidth // 128
    lprev, lv = S["dl"][slot].take_dma()
    lastap = S["dslot_lastap"][slot]
    laststv = S["dslot_lastst"][slot]

    @p.on("scalar")
    def _(e, ot=ot, qoff=qoff, qwidth=qwidth, slot=slot, lprev=lprev, lastap=lastap):
        if lastap is not None:
            e.wait_ge(lastap[0].h, lastap[1])
        if lprev:
            e.wait_ge(S["dl"][slot].h, lprev)
        e.dma_start(out=dland[:, slot * 1408:slot * 1408 + qwidth],
                    in_=bass.AP(E["Wd_f"], (ot * 128) * I_FFN + qoff,
                                [[I_FFN, 128], [1, qwidth]])).then_inc(S["dl"][slot].h, 16)

    dscv = S["dsc"].take(1)

    @p.on("vector")
    def _(e, slot=slot, lv=lv, ng=ng, qwidth=qwidth):
        e.wait_ge(S["dl"][slot].h, lv)
        e.drain()
        e.tensor_reduce(dam[:, slot * 33:slot * 33 + ng],
                        dland[:, slot * 1408:slot * 1408 + qwidth]
                        .rearrange("p (g k) -> p g k", k=128),
                        AxX, Alu.max, apply_absolute_value=True)
        e.drain()
        ts(e, dam[:, slot * 33 + 11:slot * 33 + 11 + ng],
           dam[:, slot * 33:slot * 33 + ng], 1.0 / 127.0, Alu.mult, QEPS, Alu.max)
        e.drain()
        e.reciprocal(dam[:, slot * 33 + 22:slot * 33 + 22 + ng],
                     dam[:, slot * 33 + 11:slot * 33 + 11 + ng])
        e.drain().then_inc(S["dsc"].h, 1)

    eng = "scalar" if u % 2 == 1 else "vector"
    qsem = S["dap"] if eng == "scalar" else S["dav"]
    qv = qsem.take(1)
    S["dslot_lastap"][slot] = (qsem, qv)

    if eng == "vector":

        @p.on("vector")
        def _(e, slot=slot, ng=ng, laststv=laststv):
            if laststv:
                e.wait_ge(S["dqs"].h, laststv)
            e.drain()
            for g in range(ng):
                par = g % 2
                if g >= 2:
                    e.drain()
                ts(e, qub[:, par * 128:(par + 1) * 128],
                   dland[:, slot * 1408 + g * 128:slot * 1408 + (g + 1) * 128],
                   dam[:, slot * 33 + 22 + g:slot * 33 + 22 + g + 1], Alu.mult,
                   MAGIC, Alu.add)
                e.drain()
                ts(e, dqb[:, slot * 1408 + g * 128:slot * 1408 + (g + 1) * 128],
                   qub[:, par * 128:(par + 1) * 128],
                   MAGIC, Alu.subtract,
                   dam[:, slot * 33 + 11 + g:slot * 33 + 11 + g + 1], Alu.mult)
            e.drain().then_inc(S["dav"].h, 1)
    else:

        @p.on("scalar")
        def _(e, slot=slot, ng=ng, dscv=dscv, laststv=laststv):
            e.wait_ge(S["dsc"].h, dscv)
            if laststv:
                e.wait_ge(S["dqs"].h, laststv)
            e.drain()
            for g in range(ng):
                par = g % 2
                if g >= 2:
                    e.drain()
                e.activation(qup[:, par * 128:(par + 1) * 128],
                             dland[:, slot * 1408 + g * 128:slot * 1408 + (g + 1) * 128],
                             Act.Identity, bias=E["actc"][:, 4:5],
                             scale=dam[:, slot * 33 + 22 + g:slot * 33 + 22 + g + 1])
                e.drain()
                e.activation(qup[:, 256 + par * 128:256 + (par + 1) * 128],
                             qup[:, par * 128:(par + 1) * 128],
                             Act.Identity, bias=E["actc"][:, 5:6])
                e.drain()
                e.activation(dqb[:, slot * 1408 + g * 128:slot * 1408 + (g + 1) * 128],
                             qup[:, 256 + par * 128:256 + (par + 1) * 128],
                             Act.Identity,
                             scale=dam[:, slot * 33 + 11 + g:slot * 33 + 11 + g + 1])
            e.drain().then_inc(S["dap"].h, 1)

    dsp, dsv = S["dqs"].take_dma()
    S["dslot_lastst"][slot] = dsv

    @p.on("scalar")
    def _(e, ot=ot, qoff=qoff, qwidth=qwidth, slot=slot, qsem=qsem, qv=qv, dsp=dsp):
        e.wait_ge(qsem.h, qv)
        if dsp:
            e.wait_ge(S["dqs"].h, dsp)
        e.dma_start(out=bass.AP(E["wdq_scr"], (ot * 128) * I_FFN + qoff,
                                [[I_FFN, 128], [1, qwidth]]),
                    in_=dqb[:, slot * 1408:slot * 1408 + qwidth]).then_inc(S["dqs"].h, 16)


def _p4_down(nc, E):
    PS = E["PS"]
    with contextlib.ExitStack() as _st:
        hqT = _st.enter_context(nc.sbuf_tensor("p4_hqT", [128, 86 * 1024], BF16))
        wdt = _st.enter_context(nc.sbuf_tensor("p4_wdt", [128, 4 * 512], BF16))
        r1l = _st.enter_context(nc.sbuf_tensor("p4_r1l", [128, 2 * 512], F32))
        x1p = _st.enter_context(nc.sbuf_tensor("p4_x1p", [128, 512], F32))
        r2e = _st.enter_context(nc.sbuf_tensor("p4_r2e", [128, 2 * 512], F32))

        p = Plan(nc, "p4")
        s_ht = p.sem("ht")
        s_wt = [p.sem(f"wt{i}") for i in range(4)]
        s_mm = p.sem("mm")
        s_r1 = p.sem("r1")
        s_ev = p.sem("ev")
        s_rd = p.sem("rd")

        @p.on("sync")
        def _(e):
            n = 0
            for ich in range(86):
                for qq in range(4):
                    if n:
                        e.wait_ge(s_ht.h, n)
                    e.dma_start(out=hqT[:, ich * 1024 + qq * 256:
                                        ich * 1024 + qq * 256 + 256],
                                in_=bass.AP(E["hq_scr"], (qq * 256) * I_FFN + ich * 128,
                                            [[I_FFN, 256], [1, 128]]),
                                transpose=True).then_inc(s_ht.h, 16)
                    n += 16
        s_ht.take(16 * 4 * 86)

        for oe in range(8):
            for ich in range(86):
                t = oe * 86 + ich
                slot = t % 4

                @p.on("scalar")
                def _(e, oe=oe, ich=ich, t=t, slot=slot):
                    if t >= 4:
                        e.wait_ge(s_mm.h, t - 3)
                        e.wait_ge(s_wt[slot].h, 16 * (t // 4))
                    e.dma_start(out=wdt[:, slot * 512:(slot + 1) * 512],
                                in_=bass.AP(E["wdq_scr"], (oe * 512) * I_FFN + ich * 128,
                                            [[I_FFN, 512], [1, 128]]),
                                transpose=True).then_inc(s_wt[slot].h, 16)

                s_mm.take(1)

                @p.on("tensor")
                def _(e, oe=oe, ich=ich, t=t, slot=slot):
                    if oe == 0:
                        e.wait_ge(s_ht.h, 16 * 4 * (ich + 1))
                    if ich == 0 and oe:
                        e.wait_ge(s_ev.h, oe * 8)
                    e.wait_ge(s_wt[slot].h, 16 * (t // 4 + 1))
                    last = None
                    for c in range(NCH):
                        last = e.matmul(PS[c][:, 0:512],
                                        hqT[:, ich * 1024 + c * 128:ich * 1024 + c * 128 + 128],
                                        wdt[:, slot * 512:(slot + 1) * 512],
                                        start=(ich == 0), stop=(ich == 85))
                    last.then_inc(s_mm.h, 1)

            for c in range(NCH):
                e_i = oe * 8 + c
                sl = e_i % 2

                @p.on("scalar")
                def _(e, oe=oe, c=c, e_i=e_i, sl=sl):
                    if e_i:
                        e.wait_ge(s_r1.h, 16 * e_i)
                    if e_i >= 2:
                        e.wait_ge(s_ev.h, e_i - 1)
                    e.dma_start(out=r1l[:, sl * 512:sl * 512 + 512],
                                in_=bass.AP(E["r1_scr"], (c * 128) * HID + oe * 512,
                                            [[HID, 128], [1, 512]])).then_inc(s_r1.h, 16)

                @p.on("vector")
                def _(e, oe=oe, c=c, e_i=e_i, sl=sl):
                    e.wait_ge(s_mm.h, (oe + 1) * 86)
                    e.wait_ge(s_r1.h, 16 * (e_i + 1))
                    if e_i >= 2:
                        e.wait_ge(s_rd.h, 16 * (e_i - 1))
                    e.drain()
                    ts(e, x1p[:, :], r1l[:, sl * 512:sl * 512 + 512],
                       E["ln1_mean"][:, c:c + 1], Alu.subtract,
                       E["ln1_inv"][:, c:c + 1], Alu.mult)
                    e.drain()
                    e.tensor_tensor(r2e[:, sl * 512:sl * 512 + 512],
                                    PS[c][:, 0:512], x1p[:, :], Alu.add)
                    e.drain().then_inc(s_ev.h, 1)
                s_ev.take(1)

                @p.on("scalar")
                def _(e, oe=oe, c=c, e_i=e_i, sl=sl):
                    e.wait_ge(s_ev.h, e_i + 1)
                    if e_i:
                        e.wait_ge(s_rd.h, 16 * e_i)
                    e.dma_start(out=bass.AP(E["r2_scr"], (c * 128) * HID + oe * 512,
                                            [[HID, 128], [1, 512]]),
                                in_=r2e[:, sl * 512:sl * 512 + 512]).then_inc(s_rd.h, 16)
                s_r1.take(16)
                s_rd.take(16)

        rd_fin, ev_fin = s_rd.v, s_ev.v

        @p.on("scalar")
        def _(e):
            e.wait_ge(s_rd.h, rd_fin)

        @p.on("vector")
        def _(e):
            e.wait_ge(s_ev.h, ev_fin)
        p.run()

    _p45_ln2(nc, E)


def _p45_ln2(nc, E):
    with contextlib.ExitStack() as _st:
        r2row = _st.enter_context(nc.sbuf_tensor("p45_r2", [128, 2 * HID], F32))
        obuf = _st.enter_context(nc.sbuf_tensor("p45_ob", [128, HID], F32))
        tr = _st.enter_context(nc.sbuf_tensor("p45_tr", [128, HID], F32))
        mv = _st.enter_context(nc.sbuf_tensor("p45_mv", [128, 8], F32))

        p = Plan(nc, "p45")
        s_ld = p.sem("ld")
        s_sv = p.sem("sv")
        s_vv = p.sem("vv")
        s_ob = p.sem("ob")
        s_od = p.sem("od")

        for c in range(NCH):
            sl = c % 2

            @p.on("sync")
            def _(e, c=c, sl=sl):
                if c:
                    e.wait_ge(s_ld.h, 16 * c)
                if c >= 2:
                    e.wait_ge(s_ob.h, c - 1)
                e.dma_start(out=r2row[:, sl * HID:(sl + 1) * HID],
                            in_=bass.AP(E["r2_scr"], (c * 128) * HID,
                                        [[HID, 128], [1, HID]])).then_inc(s_ld.h, 16)

            svv = s_sv.take(2)

            @p.on("scalar")
            def _(e, c=c, sl=sl):
                e.wait_ge(s_ld.h, 16 * (c + 1))
                if c:
                    e.wait_ge(s_ob.h, c)
                slc = r2row[:, sl * HID:(sl + 1) * HID]
                e.activation(tr[:, :], slc, Act.Square, accum_out=mv[:, 0:1])
                e.drain()
                e.activation(tr[:, :], slc, Act.Identity, accum_out=mv[:, 1:2])
                e.drain()
                e.activation(mv[:, 2:3], mv[:, 1:2], Act.Identity, scale=E["actc"][:, 3:4])
                e.drain()
                e.activation(mv[:, 3:4], mv[:, 2:3], Act.Square)
                e.drain()
                e.activation(mv[:, 4:5], mv[:, 0:1], Act.Identity, scale=E["actc"][:, 3:4])
                e.drain().then_inc(s_sv.h, 1)

            vvv = s_vv.take(1)

            @p.on("vector")
            def _(e, svv=svv):
                e.wait_ge(s_sv.h, svv - 1)
                e.drain()
                e.tensor_sub(mv[:, 5:6], mv[:, 4:5], mv[:, 3:4])
                e.drain().then_inc(s_vv.h, 1)

            @p.on("scalar")
            def _(e, vvv=vvv):
                e.wait_ge(s_vv.h, vvv)
                e.activation(mv[:, 6:7], mv[:, 5:6], Act.Ln, bias=E["actc"][:, 1:2])
                e.drain()
                e.activation(mv[:, 7:8], mv[:, 6:7], Act.Exp, scale=-0.5)
                e.drain().then_inc(s_sv.h, 1)

            obv = s_ob.take(1)

            @p.on("vector")
            def _(e, c=c, sl=sl, svv=svv, obv=obv):
                e.wait_ge(s_sv.h, svv)
                if obv >= 2:
                    e.wait_ge(s_od.h, 16 * (obv - 1))
                e.drain()
                ts(e, obuf[:, :], r2row[:, sl * HID:(sl + 1) * HID], mv[:, 2:3],
                   Alu.subtract, mv[:, 7:8], Alu.mult)
                e.drain().then_inc(s_ob.h, 1)

            odv = s_od.take(16)

            @p.on("sync")
            def _(e, c=c, obv=obv, odv=odv):
                e.wait_ge(s_ob.h, obv)
                if odv >= 32:
                    e.wait_ge(s_od.h, odv - 16)
                e.dma_start(out=bass.AP(E["out"], c * 128 * HID, [[HID, 128], [1, HID]]),
                            in_=obuf[:, :]).then_inc(s_od.h, 16)

        od_fin = s_od.v

        @p.on("sync")
        def _(e):
            e.wait_ge(s_od.h, od_fin)

        @p.on("vector")
        def _(e):
            e.wait_ge(s_od.h, od_fin)
        p.run()


def _dbg_dump(nc, E):
    dbg = E["dbg"]
    p = Plan(nc, "dbgd")
    s = p.sem("st")
    sv = s.take(16 * sum(1 for nm in ["dbg_G", "dbg_state0", "dbg_dt"] if nm in dbg))

    @p.on("sync")
    def _(e):
        n = 0
        for nm, src in [("dbg_G", "G_tok"), ("dbg_state0", "state"), ("dbg_dt", "dt_tok")]:
            if nm in dbg:
                if n:
                    e.wait_ge(s.h, 16 * n)
                e.dma_start(out=dbg[nm][:, :], in_=E[src][:, :]).then_inc(s.h, 16)
                n += 1
        e.wait_ge(s.h, sv)

    s2 = p.sem("st2")
    sv2 = s2.take(16 * sum(1 for nm in ["dbg_ktok", "dbg_vtok"] if nm in dbg))

    @p.on("gpsimd")
    def _(e):
        n = 0
        for nm, src in [("dbg_ktok", "k_tok"), ("dbg_vtok", "v_tok")]:
            if nm in dbg:
                if n:
                    e.wait_ge(s2.h, 16 * n)
                e.dma_start(out=dbg[nm][:, :], in_=E[src][:, :]).then_inc(s2.h, 16)
                n += 1
        e.wait_ge(s2.h, sv2)

    p.run()



def _prep_inputs(inputs, core):
    b, q = core // 4, core % 4
    x = np.asarray(inputs["L0_attn_hidden_states"])[b, q * N_TOK:(q + 1) * N_TOK, :]
    d = dict(
        xT=np.ascontiguousarray(x.T),
        x_tok=np.ascontiguousarray(x),
        WqT=np.ascontiguousarray(np.asarray(inputs["L0_attn_Wq"]).T),
        WkT=np.ascontiguousarray(np.asarray(inputs["L0_attn_Wk"]).T),
        WvT=np.ascontiguousarray(np.asarray(inputs["L0_attn_Wv"]).T),
        WgT=np.ascontiguousarray(np.asarray(inputs["L0_attn_Wg"]).T),
        WoT=np.ascontiguousarray(np.asarray(inputs["L0_attn_Wo"]).T),
        WinT=np.ascontiguousarray(np.asarray(inputs["L0_attn_Win"]).T),
        Wg_f=np.ascontiguousarray(np.asarray(inputs["L0_ffn_w_gate"])),
        Wu_f=np.ascontiguousarray(np.asarray(inputs["L0_ffn_w_up"])),
        Wd_f=np.ascontiguousarray(np.asarray(inputs["L0_ffn_w_down"])),
        dtb_t=np.tile(np.asarray(inputs["L0_attn_dt_bias"])[None, :], (128, 1)),
        alog_t=np.tile(np.asarray(inputs["L0_attn_A_log_bias"])[None, :], (128, 1)),
        gnorm_t=np.tile(np.asarray(inputs["L0_attn_gnorm_w"])[None, :], (128, 1)),
    )
    d.update(_host_consts(core))
    return {k: np.ascontiguousarray(np.asarray(v, dtype=np.float32)) for k, v in d.items()}


def kernel(**inputs):
    nc, E = build_program(depth_stop=99, debug=0)
    nc.finalize()
    in_maps = [_prep_inputs(inputs, c) for c in range(N_CORES)]
    res = run_bass_kernel_spmd(nc, in_maps, core_ids=list(range(N_CORES)))
    shards = [np.asarray(r["out"]) for r in res.results]
    full = np.empty((B, L, HID), dtype=np.float32)
    for c in range(N_CORES):
        b, q = c // 4, c % 4
        full[b, q * N_TOK:(q + 1) * N_TOK, :] = shards[c]
    return full


# version 10 (final)
# speedup vs baseline: 1.2327x; 1.2327x over previous
import contextlib
import numpy as np
import ml_dtypes

import concourse.bacc as bacc
import concourse.bass as bass
import concourse.mybir as mybir
from concourse.bass_utils import run_bass_kernel_spmd

F32 = mybir.dt.float32
BF16 = mybir.dt.bfloat16
Alu = mybir.AluOpType
Act = mybir.ActivationFunctionType
AxX = mybir.AxisListType.X

B, L, HID = 2, 4096, 4096
H, KVH, D, CHUNK = 32, 8, 128, 128
I_FFN = 11008
N_CORES = 8
N_TOK = 1024
NCH = 8
ROPE_THETA = 10000.0
RMS_EPS = 1e-5
LN_EPS = 1e-5
MAGIC = 12582912.0
QEPS = 1e-8
NEG_BIG = -1e30
NCC = H * D * D + 32

ENGS = ("sync", "scalar", "vector", "tensor", "gpsimd")


class Plan:

    def __init__(self, nc, name):
        self.nc = nc
        self.name = name
        self.q = {e: [] for e in ENGS}

    def on(self, eng):
        def deco(fn):
            self.q[eng].append(fn)
            return None
        return deco

    def sem(self, name):
        return Sem(self.nc, f"{self.name}_{name}")

    def run(self):
        with self.nc.Block(self.name) as blk:
            for eng in ENGS:
                thunks = self.q[eng]
                if thunks:
                    getattr(blk, eng)(lambda e, th=tuple(thunks): [f(e) for f in th])


class Sem:
    def __init__(self, nc, name):
        self.h = nc.alloc_semaphore(name)
        self.v = 0

    def take(self, n=1):
        self.v += n
        return self.v

    def take_dma(self):
        prev = self.v
        self.v += 16
        return prev, self.v


def ts(e, out, in0, s1, op0, s2=None, op1=Alu.bypass):
    return e.tensor_scalar(out, in0, s1, s2, op0, op1)



def _host_consts(core):
    q = core % 4
    t0 = q * N_TOK
    pos = np.arange(t0, t0 + N_TOK, dtype=np.float32)
    inv_freq = 1.0 / ROPE_THETA ** (np.arange(0, D, 2, dtype=np.float32) / D)
    ang = pos[:, None] * inv_freq
    cos = np.concatenate([np.cos(ang), np.cos(ang)], axis=1).astype(np.float32)
    sin = np.concatenate([np.sin(ang), np.sin(ang)], axis=1).astype(np.float32)
    sinS = sin.copy()
    sinS[:, :64] *= -1.0
    cos_tok = np.ascontiguousarray(
        cos.reshape(NCH, 128, D).transpose(1, 0, 2).reshape(128, NCH * D))
    sinS_tok = np.ascontiguousarray(
        sinS.reshape(NCH, 128, D).transpose(1, 0, 2).reshape(128, NCH * D))
    jj, ii = np.meshgrid(np.arange(128), np.arange(128), indexing="ij")
    maskT = np.where(jj <= ii, 0.0, NEG_BIG).astype(np.float32)
    ident = np.eye(128, dtype=np.float32)
    ones_row = np.ones((1, 128), dtype=np.float32)
    e_cols = np.zeros((128, 16), np.float32)
    b_tile = np.full((128, 128), NEG_BIG, np.float32)
    for r in range(4):
        if r < q:
            b_tile[:, r * 32:(r + 1) * 32] = 0.0
            for rp in range(r + 1, q):
                e_cols[:, r * 4 + rp] = 1.0
    return dict(cos_c=cos_tok, sinS_c=sinS_tok, mask_c=maskT, ident_c=ident,
                ones_c=ones_row, ecols_c=e_cols, btile_c=b_tile)



def build_program(depth_stop=99, debug=0):
    nc = bacc.Bacc()
    E = {"debug": debug}

    def dram(name, shape, dt=F32, kind="Internal"):
        E[name] = nc.dram_tensor(name, shape, dt, kind=kind)
        return E[name]

    for nm, shape in [
        ("xT", [HID, N_TOK]), ("x_tok", [N_TOK, HID]),
        ("WqT", [HID, H * D]), ("WkT", [HID, KVH * D]), ("WvT", [HID, KVH * D]),
        ("WgT", [HID, H * D]), ("WoT", [H * D, HID]), ("WinT", [HID, H]),
        ("Wg_f", [I_FFN, HID]), ("Wu_f", [I_FFN, HID]), ("Wd_f", [HID, I_FFN]),
        ("dtb_t", [128, H]), ("alog_t", [128, H]), ("gnorm_t", [128, D]),
        ("cos_c", [128, NCH * D]), ("sinS_c", [128, NCH * D]),
        ("mask_c", [128, 128]), ("ident_c", [128, 128]), ("ones_c", [1, 128]),
        ("ecols_c", [128, 16]), ("btile_c", [128, 128]),
    ]:
        dram(nm, shape, F32, "ExternalInput")
    dram("out", [N_TOK, HID], F32, "ExternalOutput")
    scrkind = "ExternalOutput" if debug >= 3 else "Internal"
    dram("r1_scr", [N_TOK, HID], F32, scrkind)
    dram("r2_scr", [N_TOK, HID], F32, scrkind)
    dram("hq_scr", [N_TOK, I_FFN], BF16, scrkind)
    dram("wgq_scr", [I_FFN, HID], BF16, scrkind)
    dram("wuq_scr", [I_FFN, HID], BF16, scrkind)
    dram("wdq_scr", [HID, I_FFN], BF16, scrkind)
    dram("gl_scr", [1, NCH * H], F32)
    dram("cc_in", [1, NCC], F32)
    dram("cc_out", [4, NCC], F32)

    A = nc.alloc_sbuf_tensor
    for nm, shape, dt in [
        ("cos_s", [128, NCH * D], F32), ("sinS_s", [128, NCH * D], F32),
        ("mask_s", [128, 128], F32), ("ident_s", [128, 128], F32),
        ("ones_s", [1, 128], F32), ("gnorm_s", [128, D], F32),
        ("dtb_s", [128, H], F32), ("aneg_s", [128, H], F32),
        ("ecols_s", [128, 16], F32), ("btile_s", [128, 128], F32),
        ("st_sum", [128, NCH], F32), ("st_sq", [128, NCH], F32),
        ("ln1_mean", [128, NCH], F32), ("ln1_inv", [128, NCH], F32),
        ("actc", [128, 4], F32),
    ]:
        E[nm] = A(nm, shape, dt)
    E["PS"] = [nc.alloc_psum_tensor(f"ps{i}", [128, 512], F32) for i in range(8)]

    dbg = {}
    if debug >= 1:
        for nm, shape in [("dbg_G", [128, NCH * H]), ("dbg_state0", [128, H * D]),
                          ("dbg_ktok", [128, NCH * KVH * D]), ("dbg_vtok", [128, NCH * KVH * D]),
                          ("dbg_dt", [128, NCH * H])]:
            dbg[nm] = nc.dram_tensor(nm, shape, F32, kind="ExternalOutput")
    E["dbg"] = dbg

    _p0(nc, E)
    if depth_stop >= 1:
        with (
            nc.sbuf_tensor("k_tok", [128, NCH * KVH * D], BF16) as k_tok,
            nc.sbuf_tensor("v_tok", [128, NCH * KVH * D], BF16) as v_tok,
            nc.sbuf_tensor("dt_tok", [128, NCH * H], F32) as dt_tok,
            nc.sbuf_tensor("G_tok", [128, NCH * H], F32) as G_tok,
            nc.sbuf_tensor("G_fm", [32, NCH * 128], F32) as G_fm,
            nc.sbuf_tensor("kdec_e", [128, NCH * H], F32) as kdec_e,
            nc.sbuf_tensor("cdall", [128, NCH * H], F32) as cdall,
            nc.sbuf_tensor("state", [128, H * D], F32) as state,
            nc.sbuf_tensor("glrow", [1, NCH * H], F32) as glrow,
        ):
            E.update(k_tok=k_tok, v_tok=v_tok, dt_tok=dt_tok, G_tok=G_tok,
                     G_fm=G_fm, kdec_e=kdec_e, cdall=cdall, state=state,
                     glrow=glrow)
            _p1(nc, E)
            if depth_stop >= 2:
                _exchange(nc, E)
            if depth_stop >= 3:
                for half in (0, 1):
                    _p2_half(nc, E, half)
            if debug >= 1:
                _dbg_dump(nc, E)
    if depth_stop >= 4:
        with nc.sbuf_tensor("xq_fm", [128, 32 * N_TOK], BF16) as xq_fm:
            E["xq_fm"] = xq_fm
            _p25_ln1(nc, E)
            if depth_stop >= 5:
                _p3_ffn_gu(nc, E)
        if depth_stop >= 6:
            _p4_down(nc, E)
    return nc, E



def _p0(nc, E):
    p = Plan(nc, "p0")
    s_ld = p.sem("ld")
    ldv = s_ld.take(160)

    @p.on("sync")
    def _(e):
        for i, (dst, src) in enumerate([("cos_s", "cos_c"), ("sinS_s", "sinS_c"), ("mask_s", "mask_c"),
                         ("ident_s", "ident_c"), ("ones_s", "ones_c"), ("gnorm_s", "gnorm_t"),
                         ("dtb_s", "dtb_t"), ("aneg_s", "alog_t"), ("ecols_s", "ecols_c"),
                         ("btile_s", "btile_c")]):
            if i:
                e.wait_ge(s_ld.h, 16 * i)
            e.dma_start(out=E[dst][:, :], in_=E[src][:, :]).then_inc(s_ld.h, 16)

    @p.on("scalar")
    def _(e):
        e.wait_ge(s_ld.h, ldv)
        e.activation(E["aneg_s"][:, :], E["aneg_s"][:, :], Act.Exp)
        e.drain()
        e.mul(E["aneg_s"][:, :], E["aneg_s"][:, :], -1.0)

    @p.on("vector")
    def _(e):
        e.wait_ge(s_ld.h, ldv)
        e.memset(E["st_sum"][:, :], 0.0)
        e.memset(E["st_sq"][:, :], 0.0)
        e.memset(E["actc"][:, 0:1], RMS_EPS)
        e.memset(E["actc"][:, 1:2], LN_EPS)
        e.memset(E["actc"][:, 2:3], 1.0 / 128.0)
        e.memset(E["actc"][:, 3:4], 1.0 / HID)

    p.run()



def _p1(nc, E):
    with (
        nc.sbuf_tensor("p1_xfm", [128, 32 * 512], BF16) as xfm,
        nc.sbuf_tensor("p1_wt", [128, 2 * 8 * 512], BF16) as wt,
        nc.sbuf_tensor("p1_zero32", [32, 128], F32) as zero32,
        nc.sbuf_tensor("p1_ztmp", [128, 32], F32) as ztmp,
        nc.sbuf_tensor("p1_sp", [128, 4 * 32], F32) as sp,
        nc.sbuf_tensor("p1_gla", [128, NCH * H], F32) as glast_b,
        nc.sbuf_tensor("p1_rt", [128, 2 * 128], F32) as rtmp,
        nc.sbuf_tensor("p1_xc", [128, 8 * 128], BF16) as xc_t,
        nc.sbuf_tensor("p1_kd", [128, 8 * 128], BF16) as kd_t,
    ):
        p = Plan(nc, "p1")
        S = dict(x=p.sem("x"), w=[p.sem("w0"), p.sem("w1")], mm=p.sem("mm"), ev=p.sem("ev"),
                 evs=p.sem("evs"), gv=p.sem("gv"), gt=p.sem("gt"), z=p.sem("z"),
                 pre=p.sem("pre"), cs=p.sem("cs"), up=p.sem("up"), gd=p.sem("gd"))
        bufs = dict(xfm=xfm, wt=wt, zero32=zero32, ztmp=ztmp, sp=sp,
                    glast_b=glast_b, rtmp=rtmp, xc_t=xc_t, kd_t=kd_t)

        @p.on("vector")
        def _(e):
            e.memset(zero32[:, :], 0.0)
            e.memset(E["state"][:, :], 0.0)
            e.drain()

        os_plan = [("k", "WkT", 0, 512), ("k", "WkT", 512, 512),
                   ("v", "WvT", 0, 512), ("v", "WvT", 512, 512),
                   ("win", "WinT", 0, 32)]
        gi_ref = [0]
        for half in (0, 1):
            _p1_load_xfm(p, E, S, bufs, half)
            for oi, unit in enumerate(os_plan):
                _p1_proj_os(p, E, S, bufs, half, oi, unit)
            _p1_glast_cs(p, E, S, bufs, half, gi_ref)
        p.run()


def _p1_load_xfm(p, E, S, bufs, half):
    xfm = bufs["xfm"]
    mm_before = S["mm"].v
    xprev, _ = S["x"].take_dma()

    @p.on("gpsimd")
    def _(e):
        if half == 1:
            e.wait_ge(S["mm"].h, mm_before)
        if xprev:
            e.wait_ge(S["x"].h, xprev)
        ap_in = bass.AP(E["xT"], half * 512, [[N_TOK, 128], [N_TOK * 128, 32], [1, 512]])
        e.dma_start(out=xfm[:, :].rearrange("p (a b) -> p a b", a=32),
                    in_=ap_in).then_inc(S["x"].h, 16)


def _p1_proj_os(p, E, S, bufs, half, oi, unit):
    kind, wdn, c0, wid = unit
    PS = E["PS"]
    xfm, wt = bufs["xfm"], bufs["wt"]
    wd = E[wdn]
    wcols = KVH * D if kind != "win" else H
    ev_before, evs_before = S["ev"].v, S["evs"].v
    xv = S["x"].v
    gidx0 = (half * 5 + oi) * 4

    for ig in range(4):
        gidx = gidx0 + ig
        par = gidx % 2
        wprev, wv = S["w"][par].take_dma()

        @p.on("gpsimd")
        def _(e, ig=ig, gidx=gidx, par=par, wprev=wprev):
            if gidx >= 2:
                e.wait_ge(S["mm"].h, gidx - 1)
            if wprev:
                e.wait_ge(S["w"][par].h, wprev)
            buf = par * 8
            ap_w = bass.AP(wd, (ig * 1024) * wcols + c0,
                           [[wcols, 128], [wcols * 128, 8], [1, wid]])
            dst = wt[:, buf * 512:(buf + 8) * 512].rearrange("p (a b) -> p a b", a=8)
            e.dma_start(out=dst[:, :, 0:wid], in_=ap_w).then_inc(S["w"][par].h, 16)

        S["mm"].take(1)

        @p.on("tensor")
        def _(e, ig=ig, gidx=gidx, par=par, wv=wv):
            if ig == 0:
                e.wait_ge(S["x"].h, xv)
                e.wait_ge(S["ev"].h, ev_before)
                e.wait_ge(S["evs"].h, evs_before)
            e.wait_ge(S["w"][par].h, wv)
            buf = (gidx % 2) * 8
            last = None
            for irel in range(8):
                i = ig * 8 + irel
                for c in range(4):
                    last = e.matmul(PS[c][:, 0:wid],
                                    xfm[:, i * 512 + c * 128: i * 512 + c * 128 + 128],
                                    wt[:, (buf + irel) * 512:(buf + irel) * 512 + wid],
                                    start=(i == 0), stop=(i == 31))
            last.then_inc(S["mm"].h, 1)

    mm_done = S["mm"].v

    if kind == "k":
        S["ev"].take(1)

        @p.on("vector")
        def _(e):
            e.wait_ge(S["mm"].h, mm_done)
            rtmp = bufs["rtmp"]
            for cc in range(4):
                c = half * 4 + cc
                for kv in range(4):
                    kvh = c0 // 128 + kv
                    psl = PS[cc][:, kv * 128:(kv + 1) * 128]
                    cosl = E["cos_s"][:, c * 128:(c + 1) * 128]
                    e.drain()
                    e.tensor_tensor(rtmp[:, 0:128], psl, cosl, Alu.mult)
                    e.tensor_tensor(rtmp[:, 128:192],
                                    PS[cc][:, kv * 128 + 64:kv * 128 + 128],
                                    E["sinS_s"][:, c * 128:c * 128 + 64], Alu.mult)
                    e.tensor_tensor(rtmp[:, 192:256], PS[cc][:, kv * 128:kv * 128 + 64],
                                    E["sinS_s"][:, c * 128 + 64:c * 128 + 128], Alu.mult)
                    e.drain()
                    e.tensor_tensor(
                        E["k_tok"][:, (c * KVH + kvh) * 128:(c * KVH + kvh + 1) * 128],
                        rtmp[:, 0:128], rtmp[:, 128:256], Alu.add)
            e.drain().then_inc(S["ev"].h, 1)
    elif kind == "v":
        S["evs"].take(1)

        @p.on("scalar")
        def _(e):
            e.wait_ge(S["mm"].h, mm_done)
            for cc in range(4):
                c = half * 4 + cc
                for kv in range(4):
                    kvh = c0 // 128 + kv
                    e.copy(E["v_tok"][:, (c * KVH + kvh) * 128:(c * KVH + kvh + 1) * 128],
                           PS[cc][:, kv * 128:(kv + 1) * 128])
            e.drain().then_inc(S["evs"].h, 1)
    else:
        _p1_win_evac(p, E, S, bufs, half, mm_done)


def _p1_win_evac(p, E, S, bufs, half, mm_done):
    PS = E["PS"]
    ztmp, sp, zero32 = bufs["ztmp"], bufs["sp"], bufs["zero32"]
    for cc in range(4):
        c = half * 4 + cc
        zv = S["z"].take(1)

        @p.on("vector")
        def _(e, cc=cc, c=c):
            if cc == 0:
                e.wait_ge(S["mm"].h, mm_done)
            e.drain()
            e.tensor_tensor(ztmp[:, :], PS[cc][:, 0:32], E["dtb_s"][:, :], Alu.add)
            e.drain()
            e.tensor_relu(sp[:, 96:128], ztmp[:, :])
            e.drain().then_inc(S["z"].h, 1)

        zv2 = S["z"].take(1)

        @p.on("scalar")
        def _(e, zv=zv):
            e.wait_ge(S["z"].h, zv)
            e.activation(sp[:, 0:32], ztmp[:, :], Act.Abs)
            e.drain()
            e.activation(sp[:, 32:64], sp[:, 0:32], Act.Exp, scale=-1.0)
            e.drain()
            e.activation(sp[:, 64:96], sp[:, 32:64], Act.Ln, bias=1.0)
            e.drain().then_inc(S["z"].h, 1)

        zv3 = S["z"].take(1)

        @p.on("vector")
        def _(e, c=c, zv2=zv2):
            e.wait_ge(S["z"].h, zv2)
            e.drain()
            e.tensor_tensor(E["dt_tok"][:, c * H:(c + 1) * H], sp[:, 64:96],
                            sp[:, 96:128], Alu.add)
            e.drain()
            e.tensor_tensor(ztmp[:, :], E["dt_tok"][:, c * H:(c + 1) * H],
                            E["aneg_s"][:, :], Alu.mult)
            e.drain().then_inc(S["z"].h, 1)

        gtv = S["gt"].take(1)

        @p.on("tensor")
        def _(e, zv3=zv3):
            e.wait_ge(S["z"].h, zv3)
            e.transpose(PS[4][0:32, 0:128], ztmp[:, :],
                        E["ident_s"][:, :]).then_inc(S["gt"].h, 1)

        gvv = S["gv"].take(1)

        @p.on("vector")
        def _(e, c=c, gtv=gtv):
            e.wait_ge(S["gt"].h, gtv)
            e.drain()
            e.tensor_tensor_scan(E["G_fm"][:, c * 128:(c + 1) * 128],
                                 PS[4][0:32, 0:128], zero32[:, :], 0.0,
                                 Alu.add, Alu.add)
            e.drain().then_inc(S["gv"].h, 1)

        gtv2 = S["gt"].take(1)

        @p.on("tensor")
        def _(e, c=c, gvv=gvv):
            e.wait_ge(S["gv"].h, gvv)
            e.transpose(PS[5][:, 0:32], E["G_fm"][0:32, c * 128:(c + 1) * 128],
                        E["ident_s"][0:32, 0:32]).then_inc(S["gt"].h, 1)

        last = (cc == 3)
        if last:
            S["ev"].take(1)
            S["evs"].take(1)

        @p.on("vector")
        def _(e, c=c, gtv2=gtv2, last=last):
            e.wait_ge(S["gt"].h, gtv2)
            e.drain()
            e.tensor_copy(E["G_tok"][:, c * H:(c + 1) * H], PS[5][:, 0:32])
            e.drain()
            if last:
                e.drain().then_inc(S["ev"].h, 1)
                e.drain().then_inc(S["evs"].h, 1)


def _p1_glast_cs(p, E, S, bufs, half, gi_ref):
    PS = E["PS"]
    glast_b, xc_t, kd_t, zero32 = bufs["glast_b"], bufs["xc_t"], bufs["kd_t"], bufs["zero32"]
    h0 = half * 128
    ev_now, evs_now = S["ev"].v, S["evs"].v
    gd0, gd1 = S["gd"].take_dma()
    _, gd2 = S["gd"].take_dma()

    @p.on("sync")
    def _(e):
        e.wait_ge(S["ev"].h, ev_now)
        e.wait_ge(S["evs"].h, evs_now)
        if gd0:
            e.wait_ge(S["gd"].h, gd0)
        e.dma_start(out=bass.AP(E["gl_scr"], h0, [[0, 1], [1, 128]]),
                    in_=E["G_tok"][127:128, h0:h0 + 128]).then_inc(S["gd"].h, 16)
        e.wait_ge(S["gd"].h, gd1)
        e.dma_start(out=E["glrow"][0:1, h0:h0 + 128],
                    in_=bass.AP(E["gl_scr"], h0, [[0, 1], [1, 128]])).then_inc(S["gd"].h, 16)

    gtv = S["gt"].take(1)

    @p.on("tensor")
    def _(e):
        e.wait_ge(S["gd"].h, gd2)
        e.matmul(PS[4][:, 0:128], E["ones_s"][:, :],
                 E["glrow"][0:1, h0:h0 + 128], start=True,
                 stop=True).then_inc(S["gt"].h, 1)

    gvv = S["gv"].take(1)

    @p.on("vector")
    def _(e, gtv=gtv):
        e.wait_ge(S["gt"].h, gtv)
        e.drain()
        e.tensor_copy(glast_b[:, h0:h0 + 128], PS[4][:, 0:128])
        e.drain()
        e.tensor_sub(E["kdec_e"][:, h0:h0 + 128], glast_b[:, h0:h0 + 128],
                     E["G_tok"][:, h0:h0 + 128])
        e.drain().then_inc(S["gv"].h, 1)

    decv = S["gv"].take(1)

    @p.on("scalar")
    def _(e, gvv=gvv):
        e.wait_ge(S["gv"].h, gvv)
        e.activation(E["kdec_e"][:, h0:h0 + 128], E["kdec_e"][:, h0:h0 + 128], Act.Exp)
        e.activation(E["cdall"][:, h0:h0 + 128], glast_b[:, h0:h0 + 128], Act.Exp)
        e.drain().then_inc(S["gv"].h, 1)

    for cc in range(4):
        c = half * 4 + cc
        for hgrp in range(8):
            gi = gi_ref[0]
            gi_ref[0] += 1
            prev = S["pre"].take(1)

            @p.on("vector")
            def _(e, c=c, cc=cc, hgrp=hgrp, gi=gi):
                if hgrp == 0 and cc == 0:
                    e.wait_ge(S["gv"].h, decv)
                if gi >= 2:
                    e.wait_ge(S["cs"].h, gi - 1)
                bo = (gi % 2) * 4
                e.drain()
                for hh in range(4):
                    h = hgrp * 4 + hh
                    col = c * H + h
                    ts(e, xc_t[:, (bo + hh) * 128:(bo + hh + 1) * 128],
                       E["v_tok"][:, (c * KVH + h // 4) * 128:(c * KVH + h // 4 + 1) * 128],
                       E["dt_tok"][:, col:col + 1], Alu.mult)
                    ts(e, kd_t[:, (bo + hh) * 128:(bo + hh + 1) * 128],
                       E["k_tok"][:, (c * KVH + h // 4) * 128:(c * KVH + h // 4 + 1) * 128],
                       E["kdec_e"][:, col:col + 1], Alu.mult)
                e.drain().then_inc(S["pre"].h, 1)

            csv = S["cs"].take(1)

            @p.on("tensor")
            def _(e, gi=gi, prev=prev):
                e.wait_ge(S["pre"].h, prev)
                if gi >= 2:
                    e.wait_ge(S["up"].h, gi - 1)
                bank = 4 + (gi % 2)
                bo = (gi % 2) * 4
                last = None
                for hh in range(4):
                    last = e.matmul(PS[bank][:, hh * 128:(hh + 1) * 128],
                                    kd_t[:, (bo + hh) * 128:(bo + hh + 1) * 128],
                                    xc_t[:, (bo + hh) * 128:(bo + hh + 1) * 128],
                                    start=True, stop=True)
                last.then_inc(S["cs"].h, 1)

            S["up"].take(1)

            @p.on("vector")
            def _(e, c=c, hgrp=hgrp, gi=gi, csv=csv):
                e.wait_ge(S["cs"].h, csv)
                bank = 4 + (gi % 2)
                for hh in range(4):
                    h = hgrp * 4 + hh
                    col = c * H + h
                    e.drain()
                    e.scalar_tensor_tensor(
                        E["state"][:, h * 128:(h + 1) * 128],
                        E["state"][:, h * 128:(h + 1) * 128],
                        E["cdall"][:, col:col + 1],
                        PS[bank][:, hh * 128:(hh + 1) * 128],
                        Alu.mult, Alu.add)
                e.drain().then_inc(S["up"].h, 1)



def _exchange(nc, E):
    PS = E["PS"]
    with (
        nc.sbuf_tensor("x_gsum", [1, 32], F32) as gsum,
        nc.sbuf_tensor("x_gs4", [1, 128], F32) as gs4,
        nc.sbuf_tensor("x_gsb", [128, 128], F32) as gsb,
        nc.sbuf_tensor("x_acc", [128, 128], F32) as acc,
        nc.sbuf_tensor("x_coef", [128, 128], F32) as coef,
        nc.sbuf_tensor("x_sg", [128, H * D], F32) as sg,
        nc.sbuf_tensor("x_snew", [128, H * D], F32) as snew,
    ):
        p = Plan(nc, "xchg")
        s_v = p.sem("v"); s_d = p.sem("d"); s_cc = p.sem("cc")
        s_t = p.sem("t"); s_s = p.sem("s")

        vv = s_v.take(1)

        @p.on("vector")
        def _(e):
            e.tensor_reduce(gsum[0:1, :],
                            E["glrow"][0:1, :].rearrange("p (c h) -> p h c", c=NCH),
                            AxX, Alu.add)
            e.memset(snew[:, :], 0.0)
            e.drain().then_inc(s_v.h, 1)

        _, d1 = s_d.take_dma()
        _, d2 = s_d.take_dma()

        @p.on("sync")
        def _(e):
            e.wait_ge(s_v.h, vv)
            e.dma_start(out=bass.AP(E["cc_in"], 0, [[D * H, 128], [1, D * H]]),
                        in_=E["state"][:, :]).then_inc(s_d.h, 16)
            e.wait_ge(s_d.h, d1)
            e.dma_start(out=bass.AP(E["cc_in"], H * D * D, [[0, 1], [1, 32]]),
                        in_=gsum[0:1, :]).then_inc(s_d.h, 16)

        ccv = s_cc.take(1)

        @p.on("gpsimd")
        def _(e):
            e.wait_ge(s_d.h, d2)
            e.collective_compute(
                "AllGather", Alu.bypass,
                replica_groups=[[0, 1, 2, 3], [4, 5, 6, 7]],
                ins=[E["cc_in"][:, :].opt()],
                outs=[E["cc_out"][:, :].opt()]).then_inc(s_cc.h, 1)

        d2b, d3 = s_d.take_dma()

        @p.on("sync")
        def _(e):
            e.wait_ge(s_cc.h, ccv)
            e.wait_ge(s_d.h, d2b)
            e.dma_start(out=gs4[:, :].rearrange("p (a b) -> p a b", a=4),
                        in_=bass.AP(E["cc_out"], H * D * D, [[0, 1], [NCC, 4], [1, 32]])
                        ).then_inc(s_d.h, 16)

        tv = s_t.take(1)

        @p.on("tensor")
        def _(e):
            e.wait_ge(s_d.h, d3)
            e.matmul(PS[4][:, 0:128], E["ones_s"][:, :], gs4[:, :], start=True,
                     stop=True).then_inc(s_t.h, 1)

        vv2 = s_v.take(1)

        @p.on("vector")
        def _(e):
            e.wait_ge(s_t.h, tv)
            e.drain()
            e.tensor_copy(gsb[:, :], PS[4][:, 0:128])
            e.tensor_copy(acc[:, :], E["btile_s"][:, :])
            e.drain()
            for r in range(4):
                for rp in range(4):
                    e.scalar_tensor_tensor(acc[:, r * 32:(r + 1) * 32],
                                           gsb[:, rp * 32:(rp + 1) * 32],
                                           E["ecols_s"][:, r * 4 + rp:r * 4 + rp + 1],
                                           acc[:, r * 32:(r + 1) * 32],
                                           Alu.mult, Alu.add)
                    e.drain()
            e.drain().then_inc(s_v.h, 1)

        sv = s_s.take(1)

        @p.on("scalar")
        def _(e):
            e.wait_ge(s_v.h, vv2)
            e.activation(coef[:, :], acc[:, :], Act.Exp)
            e.drain().then_inc(s_s.h, 1)

        for r in range(4):
            combine_before = s_v.v
            dvp, dvr = s_d.take_dma()

            @p.on("sync")
            def _(e, r=r, combine_before=combine_before, dvp=dvp):
                if r > 0:
                    e.wait_ge(s_v.h, combine_before)
                e.wait_ge(s_d.h, dvp)
                e.dma_start(out=sg[:, :],
                            in_=bass.AP(E["cc_out"], r * NCC, [[D * H, 128], [1, D * H]])
                            ).then_inc(s_d.h, 16)

            s_v.take(1)

            @p.on("vector")
            def _(e, r=r, dvr=dvr):
                e.wait_ge(s_d.h, dvr)
                if r == 0:
                    e.wait_ge(s_s.h, sv)
                for h in range(H):
                    e.drain()
                    e.scalar_tensor_tensor(snew[:, h * 128:(h + 1) * 128],
                                           sg[:, h * 128:(h + 1) * 128],
                                           coef[:, r * 32 + h:r * 32 + h + 1],
                                           snew[:, h * 128:(h + 1) * 128],
                                           Alu.mult, Alu.add)
                e.drain().then_inc(s_v.h, 1)

        @p.on("vector")
        def _(e):
            e.drain()
            e.tensor_copy(E["state"][:, :], snew[:, :])
            e.drain()

        p.run()


def _p2_half(nc, E, half):
    with contextlib.ExitStack() as _st:
        xfm = _st.enter_context(nc.sbuf_tensor(f"p2_xfm{half}", [128, 32 * 512], BF16))
        wt = _st.enter_context(nc.sbuf_tensor(f"p2_wt{half}", [128, 2 * 8 * 512], BF16))
        q_st = _st.enter_context(nc.sbuf_tensor(f"p2_qst{half}", [128, 4 * 8 * 128], BF16))
        g_st = _st.enter_context(nc.sbuf_tensor(f"p2_gst{half}", [128, 4 * 8 * 128], BF16))
        qfm = _st.enter_context(nc.sbuf_tensor(f"p2_qfm{half}", [128, 4 * 8 * 128], BF16))
        kfm = _st.enter_context(nc.sbuf_tensor(f"p2_kfm{half}", [128, 4 * 2 * 128], BF16))
        yr_st = _st.enter_context(nc.sbuf_tensor(f"p2_yr{half}", [128, 4 * 8 * 128], BF16))
        gy_fm = _st.enter_context(nc.sbuf_tensor(f"p2_gy{half}", [128, 32 * 4 * 128], BF16))
        rtmp = _st.enter_context(nc.sbuf_tensor(f"p2_rt{half}", [128, 256], F32))
        dec = _st.enter_context(nc.sbuf_tensor(f"p2_dec{half}", [128, 2 * 128], F32))
        expg = _st.enter_context(nc.sbuf_tensor(f"p2_expg{half}", [128, 2 * 128], F32))
        sd = _st.enter_context(nc.sbuf_tensor(f"p2_sd{half}", [128, 2 * 128], BF16))
        qg = _st.enter_context(nc.sbuf_tensor(f"p2_qg{half}", [128, 2 * 128], BF16))
        stbf = _st.enter_context(nc.sbuf_tensor(f"p2_stbf{half}", [128, 2 * 128], BF16))
        xckd = _st.enter_context(nc.sbuf_tensor(f"p2_xckd{half}", [128, 2 * 256], BF16))
        growr = _st.enter_context(nc.sbuf_tensor(f"p2_grow{half}", [1, 2 * 128], F32))
        trash = _st.enter_context(nc.sbuf_tensor(f"p2_tr{half}", [128, 512], F32))
        ms = _st.enter_context(nc.sbuf_tensor(f"p2_ms{half}", [128, 8], F32))
        sgbuf = _st.enter_context(nc.sbuf_tensor(f"p2_sg{half}", [128, 128], BF16))
        gytmp = _st.enter_context(nc.sbuf_tensor(f"p2_gyt{half}", [128, 2 * 128], BF16))
        xb = _st.enter_context(nc.sbuf_tensor(f"p2_xb{half}", [128, 2 * 512], F32))
        r1b = _st.enter_context(nc.sbuf_tensor(f"p2_r1b{half}", [128, 2 * 512], F32))
        pcol = _st.enter_context(nc.sbuf_tensor(f"p2_pcol{half}", [128, 4], F32))
        _dummy = None

        p = Plan(nc, f"p2h{half}")
        S = dict(x=p.sem("x"), w=[p.sem("w0"), p.sem("w1")], mm=p.sem("mm"),
                 ev=p.sem("ev"), tp=p.sem("tp"), pea=p.sem("pea"), va=p.sem("va"),
                 sa=p.sem("sa"), vb=p.sem("vb"), peb=p.sem("peb"), sb=p.sem("sb"),
                 vc=p.sem("vc"), sil=p.sem("sil"), gyd=p.sem("gyd"),
                 xd=p.sem("xd"), r1d=p.sem("r1d"), pre=p.sem("pre"),
                 wev=p.sem("wev"))
        bufs = dict(xfm=xfm, wt=wt, q_st=q_st, g_st=g_st, qfm=qfm, kfm=kfm,
                    yr_st=yr_st, gy_fm=gy_fm, rtmp=rtmp, dec=dec, expg=expg,
                    sd=sd, qg=qg, stbf=stbf, xckd=xckd, growr=growr,
                    trash=trash, ms=ms, sgbuf=sgbuf, gytmp=gytmp, xb=xb,
                    r1b=r1b, pcol=pcol)

        xv = S["x"].take(16)

        @p.on("gpsimd")
        def _(e):
            ap_in = bass.AP(E["xT"], half * 512, [[N_TOK, 128], [N_TOK * 128, 32], [1, 512]])
            e.dma_start(out=xfm[:, :].rearrange("p (a b) -> p a b", a=32),
                        in_=ap_in).then_inc(S["x"].h, 16)

        it_ref = [0]
        for hg in range(4):
            _p2_hg(p, E, S, bufs, half, hg, xv, it_ref)
        _p2_wo(p, E, S, bufs, half)
        r1_fin, xd_fin = S["r1d"].v, S["xd"].v

        @p.on("sync")
        def _(e):
            e.wait_ge(S["r1d"].h, r1_fin)
            e.wait_ge(S["xd"].h, xd_fin)
        p.run()


def _p2_hg(p, E, S, bufs, half, hg, xv, it_ref):
    PS = E["PS"]
    xfm, wt = bufs["xfm"], bufs["wt"]
    q_st, g_st = bufs["q_st"], bufs["g_st"]

    for osu in range(4):
        is_q = osu < 2
        wdn = "WqT" if is_q else "WgT"
        c0 = hg * 1024 + (osu % 2) * 512
        ev_before = S["ev"].v
        gidx0 = hg * 16 + osu * 4
        for ig in range(4):
            gidx = gidx0 + ig
            par = gidx % 2
            wprev, wv = S["w"][par].take_dma()

            @p.on("gpsimd")
            def _(e, ig=ig, gidx=gidx, par=par, wprev=wprev, wdn=wdn, c0=c0):
                if gidx >= 2:
                    e.wait_ge(S["mm"].h, gidx - 1)
                if wprev:
                    e.wait_ge(S["w"][par].h, wprev)
                ap_w = bass.AP(E[wdn], (ig * 1024) * (H * D) + c0,
                               [[H * D, 128], [H * D * 128, 8], [1, 512]])
                buf = par * 8
                dst = wt[:, buf * 512:(buf + 8) * 512].rearrange("p (a b) -> p a b", a=8)
                e.dma_start(out=dst, in_=ap_w).then_inc(S["w"][par].h, 16)

            S["mm"].take(1)

            @p.on("tensor")
            def _(e, ig=ig, gidx=gidx, par=par, wv=wv, ev_before=ev_before):
                if ig == 0:
                    e.wait_ge(S["x"].h, xv)
                    e.wait_ge(S["ev"].h, ev_before)
                e.wait_ge(S["w"][par].h, wv)
                buf = par * 8
                last = None
                for irel in range(8):
                    i = ig * 8 + irel
                    for cc in range(4):
                        last = e.matmul(PS[cc][:, 0:512],
                                        xfm[:, i * 512 + cc * 128: i * 512 + cc * 128 + 128],
                                        wt[:, (buf + irel) * 512:(buf + irel + 1) * 512],
                                        start=(i == 0), stop=(i == 31))
                last.then_inc(S["mm"].h, 1)

        mm_done = S["mm"].v
        S["ev"].take(1)

        if is_q:
            @p.on("vector")
            def _(e, osu=osu, hg=hg, mm_done=mm_done):
                e.wait_ge(S["mm"].h, mm_done)
                rtmp = bufs["rtmp"]
                for cc in range(4):
                    c = half * 4 + cc
                    for hr in range(4):
                        hh = (osu % 2) * 4 + hr
                        psl = PS[cc][:, hr * 128:(hr + 1) * 128]
                        cosl = E["cos_s"][:, c * 128:(c + 1) * 128]
                        e.drain()
                        e.tensor_tensor(rtmp[:, 0:128], psl, cosl, Alu.mult)
                        e.tensor_tensor(rtmp[:, 128:192],
                                        PS[cc][:, hr * 128 + 64:hr * 128 + 128],
                                        E["sinS_s"][:, c * 128:c * 128 + 64], Alu.mult)
                        e.tensor_tensor(rtmp[:, 192:256], PS[cc][:, hr * 128:hr * 128 + 64],
                                        E["sinS_s"][:, c * 128 + 64:c * 128 + 128], Alu.mult)
                        e.drain()
                        e.tensor_tensor(q_st[:, (cc * 8 + hh) * 128:(cc * 8 + hh + 1) * 128],
                                        rtmp[:, 0:128], rtmp[:, 128:256], Alu.add)
                e.drain().then_inc(S["ev"].h, 1)
        else:
            @p.on("scalar")
            def _(e, osu=osu, mm_done=mm_done):
                e.wait_ge(S["mm"].h, mm_done)
                for cc in range(4):
                    for hr in range(4):
                        hh = (osu % 2) * 4 + hr
                        e.copy(g_st[:, (cc * 8 + hh) * 128:(cc * 8 + hh + 1) * 128],
                               PS[cc][:, hr * 128:(hr + 1) * 128])
                e.drain().then_inc(S["ev"].h, 1)

    proj_done = S["ev"].v

    peb_prev = S["peb"].v
    tp_base = S["tp"].v
    tpv = S["tp"].take(16 * 40)

    @p.on("sync")
    def _(e, hg=hg, proj_done=proj_done, peb_prev=peb_prev, tp_base=tp_base):
        e.wait_ge(S["ev"].h, proj_done)
        if peb_prev:
            e.wait_ge(S["peb"].h, peb_prev)
        n = tp_base
        qfm, kfm = bufs["qfm"], bufs["kfm"]
        for cc in range(4):
            c = half * 4 + cc
            for hh in range(8):
                if n:
                    e.wait_ge(S["tp"].h, n)
                e.dma_start(out=qfm[:, (cc * 8 + hh) * 128:(cc * 8 + hh + 1) * 128],
                            in_=q_st[:, (cc * 8 + hh) * 128:(cc * 8 + hh + 1) * 128],
                            transpose=True).then_inc(S["tp"].h, 16)
                n += 16
            for kk in range(2):
                kvh = hg * 2 + kk
                if n:
                    e.wait_ge(S["tp"].h, n)
                e.dma_start(out=kfm[:, (cc * 2 + kk) * 128:(cc * 2 + kk + 1) * 128],
                            in_=E["k_tok"][:, (c * KVH + kvh) * 128:(c * KVH + kvh + 1) * 128],
                            transpose=True).then_inc(S["tp"].h, 16)
                n += 16

    for hh in range(8):
        h = hg * 8 + hh
        for cc in range(4):
            _p2_core_iter(p, E, S, bufs, half, hg, hh, h, cc, tpv,
                          first_of_hg=(hh == 0 and cc == 0), it_ref=it_ref)

    core_done_vc = S["vc"].v
    for hh in range(8):
        h = hg * 8 + hh
        for cc in range(4):
            j = hh * 4 + cc
            prev_vgy = S["vc"].v
            silv = S["sil"].take(1)

            @p.on("scalar")
            def _(e, hh=hh, cc=cc, j=j, core_done_vc=core_done_vc, prev_vgy=prev_vgy):
                if j == 0:
                    e.wait_ge(S["vc"].h, core_done_vc)
                else:
                    e.wait_ge(S["vc"].h, prev_vgy)
                e.activation(bufs["sgbuf"][:, :],
                             g_st[:, (cc * 8 + hh) * 128:(cc * 8 + hh + 1) * 128],
                             Act.Silu)
                e.drain().then_inc(S["sil"].h, 1)

            gy_jm2 = S["gyd"].v - 16 if S["gyd"].v >= 32 else 0
            vgy = S["vc"].take(1)

            @p.on("vector")
            def _(e, hh=hh, cc=cc, j=j, silv=silv, gy_jm2=gy_jm2):
                e.wait_ge(S["sil"].h, silv)
                if gy_jm2:
                    e.wait_ge(S["gyd"].h, gy_jm2)
                e.drain()
                e.tensor_tensor(bufs["gytmp"][:, (j % 2) * 128:(j % 2) * 128 + 128],
                                bufs["yr_st"][:, (cc * 8 + hh) * 128:(cc * 8 + hh + 1) * 128],
                                bufs["sgbuf"][:, :], Alu.mult)
                e.drain().then_inc(S["vc"].h, 1)

            gp, gv = S["gyd"].take_dma()

            @p.on("sync")
            def _(e, hh=hh, cc=cc, h=h, j=j, vgy=vgy, gp=gp):
                e.wait_ge(S["vc"].h, vgy)
                if gp:
                    e.wait_ge(S["gyd"].h, gp)
                e.dma_start(out=bufs["gy_fm"][:, (h * 4 + cc) * 128:(h * 4 + cc + 1) * 128],
                            in_=bufs["gytmp"][:, (j % 2) * 128:(j % 2) * 128 + 128],
                            transpose=True).then_inc(S["gyd"].h, 16)


def _p2_core_iter(p, E, S, bufs, half, hg, hh, h, cc, tpv, first_of_hg, it_ref):
    PS = E["PS"]
    it = it_ref[0]
    it_ref[0] += 1
    C = half * 4 + cc
    col = C * H + h
    par = it % 2
    bA, bB = 4 + 2 * par, 5 + 2 * par
    qsl = bufs["qfm"][:, (cc * 8 + hh) * 128:(cc * 8 + hh + 1) * 128]
    ksl = bufs["kfm"][:, (cc * 2 + (hh // 4)) * 128:(cc * 2 + (hh // 4) + 1) * 128]
    dec = bufs["dec"][:, par * 128:(par + 1) * 128]
    expg = bufs["expg"][:, par * 128:(par + 1) * 128]
    sd = bufs["sd"][:, par * 128:(par + 1) * 128]
    qg = bufs["qg"][:, par * 128:(par + 1) * 128]
    stbf = bufs["stbf"][:, par * 128:(par + 1) * 128]
    xc = bufs["xckd"][:, par * 256:par * 256 + 128]
    kd = bufs["xckd"][:, par * 256 + 128:par * 256 + 256]
    growr = bufs["growr"][0:1, par * 128:(par + 1) * 128]
    msc = bufs["ms"][:, par * 4:(par + 1) * 4]
    trash = bufs["trash"][:, 0:128]

    prev = S["pre"].take(1)
    vc_bank_free = max(0, S["vc"].v - 1)

    @p.on("vector")
    def _(e, col=col, C=C, h=h):
        if it >= 2:
            e.wait_ge(S["peb"].h, it - 1)
        e.drain()
        ts(e, xc, E["v_tok"][:, (C * KVH + h // 4) * 128:(C * KVH + h // 4 + 1) * 128],
           E["dt_tok"][:, col:col + 1], Alu.mult)
        ts(e, kd, E["k_tok"][:, (C * KVH + h // 4) * 128:(C * KVH + h // 4 + 1) * 128],
           E["kdec_e"][:, col:col + 1], Alu.mult)
        e.tensor_copy(stbf, E["state"][:, h * 128:(h + 1) * 128])
        e.drain().then_inc(S["pre"].h, 1)

    pea = S["pea"].take(1)

    @p.on("tensor")
    def _(e, col=col, prev=prev, vcb=vc_bank_free, it=it):
        if first_of_hg:
            e.wait_ge(S["tp"].h, tpv)
        e.wait_ge(S["pre"].h, prev)
        if vcb:
            e.wait_ge(S["vc"].h, vcb)
        e.transpose(PS[bA][0:1, 128:256], E["G_tok"][:, col:col + 1],
                    E["ident_s"][:, :]).then_inc(S["pea"].h, 1)

    vga = S["va"].take(1)

    @p.on("vector")
    def _(e, pea=pea):
        e.wait_ge(S["pea"].h, pea)
        e.drain()
        e.tensor_copy(growr, PS[bA][0:1, 128:256])
        e.drain().then_inc(S["va"].h, 1)

    pea2 = S["pea"].take(1)

    @p.on("tensor")
    def _(e, vga=vga):
        e.wait_ge(S["va"].h, vga)
        e.matmul(PS[bA][:, 0:128], E["ones_s"][:, :], growr, start=True, stop=True)
        e.matmul(PS[bB][:, 0:128], ksl, qsl, start=True,
                 stop=True).then_inc(S["pea"].h, 1)

    vaa = S["va"].take(1)

    @p.on("vector")
    def _(e, pea2=pea2, col=col):
        e.wait_ge(S["pea"].h, pea2)
        e.drain()
        e.scalar_tensor_tensor(dec, PS[bA][:, 0:128], E["G_tok"][:, col:col + 1],
                               E["mask_s"][:, :], Alu.subtract, Alu.add)
        e.drain().then_inc(S["va"].h, 1)

    saa = S["sa"].take(1)

    @p.on("scalar")
    def _(e, vaa=vaa):
        e.wait_ge(S["va"].h, vaa)
        e.activation(dec, dec, Act.Exp)
        e.activation(expg, PS[bA][:, 0:128], Act.Exp)
        e.drain().then_inc(S["sa"].h, 1)

    vbb = S["vb"].take(1)

    @p.on("vector")
    def _(e, saa=saa):
        e.wait_ge(S["sa"].h, saa)
        e.drain()
        e.tensor_tensor(sd, PS[bB][:, 0:128], dec, Alu.mult)
        e.tensor_tensor(qg, qsl, expg, Alu.mult)
        e.drain().then_inc(S["vb"].h, 1)

    peb = S["peb"].take(1)

    @p.on("tensor")
    def _(e, vbb=vbb):
        e.wait_ge(S["vb"].h, vbb)
        e.matmul(PS[bA][:, 256:384], sd, xc, start=True, stop=False)
        e.matmul(PS[bA][:, 256:384], qg, stbf, start=False, stop=True)
        e.matmul(PS[bB][:, 256:384], kd, xc, start=True,
                 stop=True).then_inc(S["peb"].h, 1)

    sbb = S["sb"].take(1)

    @p.on("scalar")
    def _(e, peb=peb):
        e.wait_ge(S["peb"].h, peb)
        e.activation(trash, PS[bA][:, 256:384], Act.Square, accum_out=msc[:, 0:1])
        e.drain()
        e.activation(msc[:, 1:2], msc[:, 0:1], Act.Ln, scale=E["actc"][:, 2:3],
                     bias=E["actc"][:, 0:1])
        e.drain()
        e.activation(msc[:, 2:3], msc[:, 1:2], Act.Exp, scale=-0.5)
        e.drain().then_inc(S["sb"].h, 1)

    S["vc"].take(1)

    @p.on("vector")
    def _(e, sbb=sbb, col=col, h=h, hh=hh, cc=cc):
        e.wait_ge(S["sb"].h, sbb)
        e.drain()
        ts(e, bufs["rtmp"][:, 0:128], PS[bA][:, 256:384], msc[:, 2:3], Alu.mult)
        e.drain()
        e.tensor_tensor(bufs["yr_st"][:, (cc * 8 + hh) * 128:(cc * 8 + hh + 1) * 128],
                        bufs["rtmp"][:, 0:128], E["gnorm_s"][:, :], Alu.mult)
        e.scalar_tensor_tensor(E["state"][:, h * 128:(h + 1) * 128],
                               E["state"][:, h * 128:(h + 1) * 128],
                               E["cdall"][:, col:col + 1],
                               PS[bB][:, 256:384], Alu.mult, Alu.add)
        e.drain().then_inc(S["vc"].h, 1)


def _p2_wo(p, E, S, bufs, half):
    PS = E["PS"]
    gy_fm, wt = bufs["gy_fm"], bufs["wt"]
    xb, r1b, trash, pcol = bufs["xb"], bufs["r1b"], bufs["trash"], bufs["pcol"]
    gy_done = S["gyd"].v
    mm0 = S["mm"].v

    for osu in range(8):
        o0 = osu * 512
        wo_ev_before = S["wev"].v
        gidx0 = 64 + osu * 4
        for ig in range(4):
            gidx = gidx0 + ig
            par = gidx % 2
            wprev, wv = S["w"][par].take_dma()

            @p.on("gpsimd")
            def _(e, ig=ig, gidx=gidx, par=par, wprev=wprev, o0=o0):
                e.wait_ge(S["mm"].h, gidx - 1)
                if wprev:
                    e.wait_ge(S["w"][par].h, wprev)
                ap_w = bass.AP(E["WoT"], (ig * 1024) * HID + o0,
                               [[HID, 128], [HID * 128, 8], [1, 512]])
                buf = par * 8
                dst = wt[:, buf * 512:(buf + 8) * 512].rearrange("p (a b) -> p a b", a=8)
                e.dma_start(out=dst, in_=ap_w).then_inc(S["w"][par].h, 16)

            S["mm"].take(1)

            @p.on("tensor")
            def _(e, ig=ig, gidx=gidx, par=par, wv=wv, wo_ev_before=wo_ev_before,
                  gy_done=gy_done):
                if ig == 0:
                    e.wait_ge(S["gyd"].h, gy_done)
                    e.wait_ge(S["wev"].h, wo_ev_before)
                e.wait_ge(S["w"][par].h, wv)
                buf = par * 8
                last = None
                for hrel in range(8):
                    hd = ig * 8 + hrel
                    for cc in range(4):
                        last = e.matmul(
                            PS[cc][:, 0:512],
                            gy_fm[:, (hd * 4 + cc) * 128:(hd * 4 + cc + 1) * 128],
                            wt[:, (buf + hrel) * 512:(buf + hrel + 1) * 512],
                            start=(hd == 0), stop=(hd == 31))
                last.then_inc(S["mm"].h, 1)

        mm_done = S["mm"].v

        for cc in range(4):
            C = half * 4 + cc
            xp, xv2 = S["xd"].take_dma()

            vc_now = S["vc"].v

            @p.on("sync")
            def _(e, cc=cc, C=C, osu=osu, xp=xp, vc_now=vc_now):
                if xp:
                    e.wait_ge(S["xd"].h, xp)
                if osu or cc:
                    e.wait_ge(S["vc"].h, vc_now)
                e.dma_start(out=xb[:, (cc % 2) * 512:(cc % 2) * 512 + 512],
                            in_=bass.AP(E["x_tok"], (C * 128) * HID + osu * 512,
                                        [[HID, 128], [1, 512]])).then_inc(S["xd"].h, 16)

            r1_now = S["r1d"].v
            wvv = S["vc"].take(1)

            @p.on("vector")
            def _(e, cc=cc, mm_done=mm_done, xv2=xv2, r1_now=r1_now):
                e.wait_ge(S["mm"].h, mm_done)
                e.wait_ge(S["xd"].h, xv2)
                if r1_now >= 32:
                    e.wait_ge(S["r1d"].h, r1_now - 16)
                e.drain()
                e.tensor_tensor(r1b[:, (cc % 2) * 512:(cc % 2) * 512 + 512],
                                PS[cc][:, 0:512],
                                xb[:, (cc % 2) * 512:(cc % 2) * 512 + 512], Alu.add)
                e.drain().then_inc(S["vc"].h, 1)

            sbv = S["sb"].take(1)

            @p.on("scalar")
            def _(e, cc=cc, wvv=wvv):
                e.wait_ge(S["vc"].h, wvv)
                sl = r1b[:, (cc % 2) * 512:(cc % 2) * 512 + 512]
                e.activation(trash[:, 0:512], sl, Act.Square, accum_out=pcol[:, 0:1])
                e.drain()
                e.activation(trash[:, 0:512], sl, Act.Identity, accum_out=pcol[:, 1:2])
                e.drain().then_inc(S["sb"].h, 1)

            wev2 = S["wev"].take(1)

            @p.on("vector")
            def _(e, cc=cc, C=C, sbv=sbv):
                e.wait_ge(S["sb"].h, sbv)
                e.drain()
                e.tensor_tensor(E["st_sq"][:, C:C + 1], E["st_sq"][:, C:C + 1],
                                pcol[:, 0:1], Alu.add)
                e.tensor_tensor(E["st_sum"][:, C:C + 1], E["st_sum"][:, C:C + 1],
                                pcol[:, 1:2], Alu.add)
                e.drain().then_inc(S["wev"].h, 1)

            r1p, r1v = S["r1d"].take_dma()

            @p.on("sync")
            def _(e, cc=cc, C=C, osu=osu, wvv=wvv, r1p=r1p):
                e.wait_ge(S["vc"].h, wvv)
                if r1p:
                    e.wait_ge(S["r1d"].h, r1p)
                e.dma_start(out=bass.AP(E["r1_scr"], (C * 128) * HID + osu * 512,
                                        [[HID, 128], [1, 512]]),
                            in_=r1b[:, (cc % 2) * 512:(cc % 2) * 512 + 512]
                            ).then_inc(S["r1d"].h, 16)


def _p25_ln1(nc, E):
    PS = E["PS"]
    with (
        nc.sbuf_tensor("p25_r1", [128, HID], F32) as r1buf,
        nc.sbuf_tensor("p25_x1", [128, HID], F32) as x1,
        nc.sbuf_tensor("p25_xq", [128, HID], BF16) as xq,
        nc.sbuf_tensor("p25_am", [128, 32], F32) as am,
        nc.sbuf_tensor("p25_sc", [128, 32], F32) as sc,
        nc.sbuf_tensor("p25_rc", [128, 32], F32) as rc,
        nc.sbuf_tensor("p25_u", [128, 128], F32) as ubuf,
        nc.sbuf_tensor("p25_mv", [128, 8], F32) as mv,
    ):
        p = Plan(nc, "p25")
        s_d = p.sem("d"); s_v = p.sem("v"); s_s = p.sem("s"); s_t = p.sem("t")
        for c in range(NCH):
            dp, dv = s_d.take_dma()

            @p.on("sync")
            def _(e, c=c, dp=dp):
                if dp:
                    e.wait_ge(s_d.h, dp)
                if c >= 1:
                    e.wait_ge(s_t.h, 16 * 32 * c)
                e.dma_start(out=r1buf[:, :],
                            in_=bass.AP(E["r1_scr"], c * 128 * HID, [[HID, 128], [1, HID]])
                            ).then_inc(s_d.h, 16)

            sv = s_s.take(1)

            @p.on("scalar")
            def _(e, c=c, dv=dv):
                e.wait_ge(s_d.h, dv)
                e.activation(mv[:, 0:1], E["st_sum"][:, c:c + 1], Act.Identity,
                             scale=E["actc"][:, 3:4])
                e.drain()
                e.activation(mv[:, 1:2], mv[:, 0:1], Act.Square)
                e.drain()
                e.activation(mv[:, 2:3], E["st_sq"][:, c:c + 1], Act.Identity,
                             scale=E["actc"][:, 3:4])
                e.drain().then_inc(s_s.h, 1)

            vv = s_v.take(1)

            @p.on("vector")
            def _(e, c=c, sv=sv):
                e.wait_ge(s_s.h, sv)
                e.drain()
                e.tensor_sub(mv[:, 3:4], mv[:, 2:3], mv[:, 1:2])
                e.drain().then_inc(s_v.h, 1)

            sv2 = s_s.take(1)

            @p.on("scalar")
            def _(e, c=c, vv=vv):
                e.wait_ge(s_v.h, vv)
                e.activation(mv[:, 4:5], mv[:, 3:4], Act.Ln, bias=E["actc"][:, 1:2])
                e.drain()
                e.activation(mv[:, 5:6], mv[:, 4:5], Act.Exp, scale=-0.5)
                e.drain()
                e.copy(E["ln1_mean"][:, c:c + 1], mv[:, 0:1])
                e.copy(E["ln1_inv"][:, c:c + 1], mv[:, 5:6])
                e.drain().then_inc(s_s.h, 1)

            vv2 = s_v.take(1)

            @p.on("vector")
            def _(e, c=c, sv2=sv2):
                e.wait_ge(s_s.h, sv2)
                if c >= 1:
                    e.wait_ge(s_t.h, 16 * 32 * c)
                e.drain()
                ts(e, x1[:, :], r1buf[:, :], mv[:, 0:1], Alu.subtract, mv[:, 5:6], Alu.mult)
                e.drain()
                e.tensor_reduce(am[:, :], x1[:, :].rearrange("p (g k) -> p g k", g=32),
                                AxX, Alu.max, apply_absolute_value=True)
                e.drain()
                ts(e, sc[:, :], am[:, :], 1.0 / 127.0, Alu.mult, QEPS, Alu.max)
                e.drain()
                e.reciprocal(rc[:, :], sc[:, :])
                e.drain()
                for g in range(32):
                    ts(e, ubuf[:, :], x1[:, g * 128:(g + 1) * 128], rc[:, g:g + 1],
                       Alu.mult, MAGIC, Alu.add)
                    e.drain()
                    ts(e, xq[:, g * 128:(g + 1) * 128], ubuf[:, :], MAGIC,
                       Alu.subtract, sc[:, g:g + 1], Alu.mult)
                    e.drain()
                e.drain().then_inc(s_v.h, 1)

            for g in range(32):
                tp, tv = s_t.take_dma()

                @p.on("sync")
                def _(e, c=c, g=g, vv2=vv2, tp=tp):
                    if g == 0:
                        e.wait_ge(s_v.h, vv2)
                    if tp:
                        e.wait_ge(s_t.h, tp)
                    e.dma_start(out=E["xq_fm"][:, (g * 8 + c) * 128:(g * 8 + c + 1) * 128],
                                in_=xq[:, g * 128:(g + 1) * 128],
                                transpose=True).then_inc(s_t.h, 16)
        t_fin, d_fin = s_t.v, s_d.v

        @p.on("sync")
        def _(e):
            e.wait_ge(s_t.h, t_fin)
            e.wait_ge(s_d.h, d_fin)
        p.run()


def _quant_tile(e, E, land, out_bf, am, sc, rc, ngroups, ubuf):
    e.drain()
    e.tensor_reduce(am[:, 0:ngroups],
                    land.rearrange("p (g k) -> p g k", k=128),
                    AxX, Alu.max, apply_absolute_value=True)
    e.drain()
    ts(e, sc[:, 0:ngroups], am[:, 0:ngroups], 1.0 / 127.0, Alu.mult, QEPS, Alu.max)
    e.drain()
    e.reciprocal(rc[:, 0:ngroups], sc[:, 0:ngroups])
    e.drain()
    for g in range(ngroups):
        par = g % 2
        if g >= 2:
            e.drain()
        ts(e, ubuf[:, par * 128:(par + 1) * 128], land[:, g * 128:(g + 1) * 128],
           rc[:, g:g + 1], Alu.mult, MAGIC, Alu.add)
        e.drain()
        ts(e, out_bf[:, g * 128:(g + 1) * 128], ubuf[:, par * 128:(par + 1) * 128],
           MAGIC, Alu.subtract, sc[:, g:g + 1], Alu.mult)
    e.drain()


def _p3_ffn_gu(nc, E):
    PS = E["PS"]
    with contextlib.ExitStack() as _st:
        wland = _st.enter_context(nc.sbuf_tensor("p3_wland", [128, 4 * 2048], F32))
        wqb = _st.enter_context(nc.sbuf_tensor("p3_wqb", [128, 4 * 2048], BF16))
        wtb = _st.enter_context(nc.sbuf_tensor("p3_wtb", [128, 32 * 512], BF16))
        sil = _st.enter_context(nc.sbuf_tensor("p3_sil", [128, 8 * 512], F32))
        hid = _st.enter_context(nc.sbuf_tensor("p3_hid", [128, 2 * 512], F32))
        hqb = _st.enter_context(nc.sbuf_tensor("p3_hqb", [128, 2 * 512], BF16))
        am = _st.enter_context(nc.sbuf_tensor("p3_am", [128, 16], F32))
        scb = _st.enter_context(nc.sbuf_tensor("p3_scb", [128, 4 * 16], F32))
        rcb = _st.enter_context(nc.sbuf_tensor("p3_rcb", [128, 4 * 16], F32))
        ham = _st.enter_context(nc.sbuf_tensor("p3_ham", [128, 12], F32))
        dland = _st.enter_context(nc.sbuf_tensor("p3_dland", [128, 2 * 1408], F32))
        dqb = _st.enter_context(nc.sbuf_tensor("p3_dqb", [128, 2 * 1408], BF16))
        dam = _st.enter_context(nc.sbuf_tensor("p3_dam", [128, 2 * 33], F32))
        qub = _st.enter_context(nc.sbuf_tensor("p3_qub", [128, 256], F32))
        qup = _st.enter_context(nc.sbuf_tensor("p3_qup", [128, 256], F32))

        E["p3_qub"] = qub
        p = Plan(nc, "p3")
        S = dict(wl=[p.sem(f"wl{i}") for i in range(4)], qsc=p.sem("qsc"),
                 qav=p.sem("qav"), qap=p.sem("qap"),
                 qs=p.sem("qs"), trd=p.sem("trd"), mm=p.sem("mm"),
                 sil=p.sem("sil"), hv=p.sem("hv"), hqd=p.sem("hqd"),
                 dl=[p.sem("dl0"), p.sem("dl1")], dsc=p.sem("dsc"),
                 dav=p.sem("dav"), dap=p.sem("dap"), dqs=p.sem("dqs"))
        S["hcnt"] = [0]
        S["slot_lastap"] = [None] * 4
        S["slot_lastst"] = [0] * 4
        S["ucnt"] = [0]
        S["dslot_lastap"] = [None] * 2
        S["dslot_lastst"] = [0] * 2

        wd_qwidths = [1408, 1408, 1408, 1408, 1408, 1408, 1280, 1280]
        wd_qoffs = [0, 1408, 2816, 4224, 5632, 7040, 8448, 9728]
        wd_units = [(ot, qt) for ot in range(32) for qt in range(8)]
        wd_i = [0]

        for og in range(22):
            width = 512 if og < 21 else 256
            ntile = 4 if og < 21 else 2
            _p3_quant_mat(p, E, S, wland, wqb, am, scb, rcb, qub, qup,
                          og, ntile, "Wg_f", "wgq_scr")
            _p3_mm_mat(p, E, S, wtb, sil, hid, hqb, ham, qub,
                       og, width, "wgq_scr", "g")
            _p3_quant_mat(p, E, S, wland, wqb, am, scb, rcb, qub, qup,
                          og, ntile, "Wu_f", "wuq_scr")
            nwd = 12 if og < 21 else (len(wd_units) - wd_i[0])
            for _ in range(nwd):
                if wd_i[0] >= len(wd_units):
                    break
                ot, qt = wd_units[wd_i[0]]
                wd_i[0] += 1
                _p3_wd_unit(p, E, S, dland, dqb, dam, qub, qup, ot, qt,
                            wd_qoffs[qt], wd_qwidths[qt])
            _p3_mm_mat(p, E, S, wtb, sil, hid, hqb, ham, qub,
                       og, width, "wuq_scr", "u")

        fin = [S["hqd"].v, S["dqs"].v, S["qs"].v]

        @p.on("sync")
        def _(e):
            e.wait_ge(S["hqd"].h, fin[0])
            e.wait_ge(S["dqs"].h, fin[1])
            e.wait_ge(S["qs"].h, fin[2])
        p.run()


def _p3_quant_mat(p, E, S, wland, wqb, am, scb, rcb, qub, qup, og, ntile,
                  wdram, wscr):
    qs_base = S["qs"].v
    halves = []
    for i, (ot, half) in enumerate([(o, hf) for o in range(ntile) for hf in range(2)]):
        h = S["hcnt"][0]
        S["hcnt"][0] += 1
        slot = h % 4
        eng = "vector"
        qsem = S["qav"]
        qv = qsem.take(1)
        lprev, lv = S["wl"][slot].take_dma()
        lastap = S["slot_lastap"][slot]
        laststv = S["slot_lastst"][slot]
        S["slot_lastap"][slot] = (qsem, qv)
        S["slot_lastst"][slot] = qs_base + 16 * (i + 1)
        halves.append((h, ot, half, slot, eng, qsem, qv, lv, lastap, laststv))

    for h, ot, half, slot, eng, qsem, qv, lv, lastap, laststv in halves:

        @p.on("sync")
        def _(e, og=og, ot=ot, half=half, slot=slot, lv=lv,
              lastap=lastap, wdram=wdram):
            if lastap is not None:
                e.wait_ge(lastap[0].h, lastap[1])
            if lv > 16:
                e.wait_ge(S["wl"][slot].h, lv - 16)
            e.dma_start(out=wland[:, slot * 2048:(slot + 1) * 2048],
                        in_=bass.AP(E[wdram], (og * 512 + ot * 128) * HID + half * 2048,
                                    [[HID, 128], [1, 2048]])
                        ).then_inc(S["wl"][slot].h, 16)

    for h, ot, half, slot, eng, qsem, qv, lv, lastap, laststv in halves:
        qscv = S["qsc"].take(1)

        @p.on("vector")
        def _(e, slot=slot, lv=lv):
            e.wait_ge(S["wl"][slot].h, lv)
            e.drain()
            e.tensor_reduce(am[:, 0:16],
                            wland[:, slot * 2048:(slot + 1) * 2048]
                            .rearrange("p (g k) -> p g k", k=128),
                            AxX, Alu.max, apply_absolute_value=True)
            e.drain()
            ts(e, scb[:, slot * 16:slot * 16 + 16], am[:, 0:16],
               1.0 / 127.0, Alu.mult, QEPS, Alu.max)
            e.drain()
            e.reciprocal(rcb[:, slot * 16:slot * 16 + 16],
                         scb[:, slot * 16:slot * 16 + 16])
            e.drain().then_inc(S["qsc"].h, 1)

        ub = qub if eng == "vector" else qup

        @p.on(eng)
        def _(e, slot=slot, qscv=qscv, laststv=laststv, ub=ub, eng=eng,
              lv=lv, qsem=qsem):
            if eng == "gpsimd":
                e.wait_ge(S["qsc"].h, qscv)
                e.wait_ge(S["wl"][slot].h, lv)
            if laststv:
                e.wait_ge(S["qs"].h, laststv)
            e.drain()
            for g in range(16):
                par = g % 2
                if g >= 2:
                    e.drain()
                ts(e, ub[:, par * 128:(par + 1) * 128],
                   wland[:, slot * 2048 + g * 128:slot * 2048 + (g + 1) * 128],
                   rcb[:, slot * 16 + g:slot * 16 + g + 1], Alu.mult,
                   MAGIC, Alu.add)
                e.drain()
                ts(e, wqb[:, slot * 2048 + g * 128:slot * 2048 + (g + 1) * 128],
                   ub[:, par * 128:(par + 1) * 128],
                   MAGIC, Alu.subtract,
                   scb[:, slot * 16 + g:slot * 16 + g + 1], Alu.mult)
            e.drain().then_inc(qsem.h, 1)

    for h, ot, half, slot, eng, qsem, qv, lv, lastap, laststv in halves:
        qsp, qsv = S["qs"].take_dma()

        @p.on("sync")
        def _(e, og=og, ot=ot, half=half, slot=slot, qsem=qsem, qv=qv,
              qsp=qsp, wscr=wscr):
            e.wait_ge(qsem.h, qv)
            if qsp:
                e.wait_ge(S["qs"].h, qsp)
            e.dma_start(out=bass.AP(E[wscr], (og * 512 + ot * 128) * HID + half * 2048,
                                    [[HID, 128], [1, 2048]]),
                        in_=wqb[:, slot * 2048:(slot + 1) * 2048]
                        ).then_inc(S["qs"].h, 16)


def _p3_mm_mat(p, E, S, wtb, sil, hid, hqb, ham, qub, og, width, wscr, mat):
    PS = E["PS"]
    qs_done = S["qs"].v
    m = S["mm"].v // 2
    sil_before = S["sil"].v
    hv_before = S["hv"].v

    trd_base = S["trd"].v
    trd_half = trd_base + 16 * 16
    trd_fin = S["trd"].take(16 * 32)

    for wave in range(2):

        @p.on("scalar")
        def _(e, og=og, width=width, wscr=wscr, qs_done=qs_done, wave=wave,
              trd_base=trd_base, m=m):
            if wave == 0:
                e.wait_ge(S["qs"].h, qs_done)
                if m:
                    e.wait_ge(S["mm"].h, 2 * m - 1)
            else:
                if m:
                    e.wait_ge(S["mm"].h, 2 * m)
            n = trd_base + 16 * 16 * wave
            for ich in range(16 * wave, 16 * wave + 16):
                if n:
                    e.wait_ge(S["trd"].h, n)
                e.dma_start(out=wtb[:, ich * 512:ich * 512 + width],
                            in_=bass.AP(E[wscr], (og * 512) * HID + ich * 128,
                                        [[HID, width], [1, 128]]),
                            transpose=True).then_inc(S["trd"].h, 16)
                n += 16

    S["mm"].take(1)
    mmv = S["mm"].take(1)

    for wave in range(2):

        @p.on("tensor")
        def _(e, og=og, width=width, wave=wave, trd_half=trd_half,
              trd_fin=trd_fin, sil_before=sil_before, hv_before=hv_before,
              mat=mat):
            e.wait_ge(S["trd"].h, trd_half if wave == 0 else trd_fin)
            if wave == 0:
                if mat == "g":
                    if hv_before:
                        e.wait_ge(S["hv"].h, hv_before)
                else:
                    e.wait_ge(S["sil"].h, sil_before)
            last = None
            for i in range(16 * wave, 16 * wave + 16):
                for c in range(8):
                    last = e.matmul(PS[c][:, 0:width],
                                    E["xq_fm"][:, (i * 8 + c) * 128:(i * 8 + c + 1) * 128],
                                    wtb[:, i * 512:i * 512 + width],
                                    start=(i == 0), stop=(i == 31))
            last.then_inc(S["mm"].h, 1)

    if mat == "g":
        hv_now = S["hv"].v
        S["sil"].take(1)

        @p.on("scalar")
        def _(e, og=og, width=width, mmv=mmv, hv_now=hv_now):
            e.wait_ge(S["mm"].h, mmv)
            if hv_now:
                e.wait_ge(S["hv"].h, hv_now)
            for c in range(8):
                e.activation(sil[:, c * 512:c * 512 + width], PS[c][:, 0:width], Act.Silu)
            e.drain().then_inc(S["sil"].h, 1)
    else:
        for c in range(8):
            par = c % 2
            hq_prev = S["hqd"].v - 16 if S["hqd"].v >= 32 else 0
            hvv = S["hv"].take(1)

            @p.on("vector")
            def _(e, c=c, og=og, width=width, par=par, mmv=mmv, hq_prev=hq_prev):
                if c == 0:
                    e.wait_ge(S["mm"].h, mmv)
                if hq_prev:
                    e.wait_ge(S["hqd"].h, hq_prev)
                e.drain()
                e.tensor_tensor(hid[:, par * 512:par * 512 + width],
                                sil[:, c * 512:c * 512 + width],
                                PS[c][:, 0:width], Alu.mult)
                _quant_tile(e, E, hid[:, par * 512:par * 512 + width],
                            hqb[:, par * 512:par * 512 + width],
                            ham[:, 0:4], ham[:, 4:8], ham[:, 8:12],
                            width // 128, qub)
                e.drain().then_inc(S["hv"].h, 1)

            hqp, hqv = S["hqd"].take_dma()

            @p.on("sync")
            def _(e, c=c, og=og, width=width, par=par, hvv=hvv, hqp=hqp):
                e.wait_ge(S["hv"].h, hvv)
                if hqp:
                    e.wait_ge(S["hqd"].h, hqp)
                e.dma_start(out=bass.AP(E["hq_scr"], (c * 128) * I_FFN + og * 512,
                                        [[I_FFN, 128], [1, width]]),
                            in_=hqb[:, par * 512:par * 512 + width]
                            ).then_inc(S["hqd"].h, 16)


def _p3_wd_unit(p, E, S, dland, dqb, dam, qub, qup, ot, qt, qoff, qwidth):
    u = S["ucnt"][0]
    S["ucnt"][0] += 1
    slot = u % 2
    ng = qwidth // 128
    lprev, lv = S["dl"][slot].take_dma()
    lastap = S["dslot_lastap"][slot]
    laststv = S["dslot_lastst"][slot]

    @p.on("sync")
    def _(e, ot=ot, qoff=qoff, qwidth=qwidth, slot=slot, lprev=lprev, lastap=lastap):
        if lastap is not None:
            e.wait_ge(lastap[0].h, lastap[1])
        if lprev:
            e.wait_ge(S["dl"][slot].h, lprev)
        e.dma_start(out=dland[:, slot * 1408:slot * 1408 + qwidth],
                    in_=bass.AP(E["Wd_f"], (ot * 128) * I_FFN + qoff,
                                [[I_FFN, 128], [1, qwidth]])).then_inc(S["dl"][slot].h, 16)

    dscv = S["dsc"].take(1)

    @p.on("vector")
    def _(e, slot=slot, lv=lv, ng=ng, qwidth=qwidth):
        e.wait_ge(S["dl"][slot].h, lv)
        e.drain()
        e.tensor_reduce(dam[:, slot * 33:slot * 33 + ng],
                        dland[:, slot * 1408:slot * 1408 + qwidth]
                        .rearrange("p (g k) -> p g k", k=128),
                        AxX, Alu.max, apply_absolute_value=True)
        e.drain()
        ts(e, dam[:, slot * 33 + 11:slot * 33 + 11 + ng],
           dam[:, slot * 33:slot * 33 + ng], 1.0 / 127.0, Alu.mult, QEPS, Alu.max)
        e.drain()
        e.reciprocal(dam[:, slot * 33 + 22:slot * 33 + 22 + ng],
                     dam[:, slot * 33 + 11:slot * 33 + 11 + ng])
        e.drain().then_inc(S["dsc"].h, 1)

    eng = "vector"
    qsem = S["dav"]
    ub = qub if eng == "vector" else qup
    qv = qsem.take(1)
    S["dslot_lastap"][slot] = (qsem, qv)

    @p.on(eng)
    def _(e, slot=slot, ng=ng, dscv=dscv, laststv=laststv, ub=ub, eng=eng):
        if eng == "gpsimd":
            e.wait_ge(S["dsc"].h, dscv)
        if laststv:
            e.wait_ge(S["dqs"].h, laststv)
        e.drain()
        for g in range(ng):
            par = g % 2
            if g >= 2:
                e.drain()
            ts(e, ub[:, par * 128:(par + 1) * 128],
               dland[:, slot * 1408 + g * 128:slot * 1408 + (g + 1) * 128],
               dam[:, slot * 33 + 22 + g:slot * 33 + 22 + g + 1], Alu.mult,
               MAGIC, Alu.add)
            e.drain()
            ts(e, dqb[:, slot * 1408 + g * 128:slot * 1408 + (g + 1) * 128],
               ub[:, par * 128:(par + 1) * 128],
               MAGIC, Alu.subtract,
               dam[:, slot * 33 + 11 + g:slot * 33 + 11 + g + 1], Alu.mult)
        e.drain().then_inc(qsem.h, 1)

    dsp, dsv = S["dqs"].take_dma()
    S["dslot_lastst"][slot] = dsv

    @p.on("sync")
    def _(e, ot=ot, qoff=qoff, qwidth=qwidth, slot=slot, qsem=qsem, qv=qv, dsp=dsp):
        e.wait_ge(qsem.h, qv)
        if dsp:
            e.wait_ge(S["dqs"].h, dsp)
        e.dma_start(out=bass.AP(E["wdq_scr"], (ot * 128) * I_FFN + qoff,
                                [[I_FFN, 128], [1, qwidth]]),
                    in_=dqb[:, slot * 1408:slot * 1408 + qwidth]).then_inc(S["dqs"].h, 16)


def _p4_down(nc, E):
    PS = E["PS"]
    with contextlib.ExitStack() as _st:
        hqT = _st.enter_context(nc.sbuf_tensor("p4_hqT", [128, 86 * 1024], BF16))
        wdt = _st.enter_context(nc.sbuf_tensor("p4_wdt", [128, 4 * 512], BF16))
        r1l = _st.enter_context(nc.sbuf_tensor("p4_r1l", [128, 2 * 512], F32))
        x1p = _st.enter_context(nc.sbuf_tensor("p4_x1p", [128, 512], F32))
        r2e = _st.enter_context(nc.sbuf_tensor("p4_r2e", [128, 2 * 512], F32))

        p = Plan(nc, "p4")
        s_ht = p.sem("ht")
        s_wt = [p.sem(f"wt{i}") for i in range(4)]
        s_mm = p.sem("mm")
        s_r1 = p.sem("r1")
        s_ev = p.sem("ev")
        s_rd = p.sem("rd")

        @p.on("sync")
        def _(e):
            n = 0
            for ich in range(86):
                for qq in range(4):
                    if n:
                        e.wait_ge(s_ht.h, n)
                    e.dma_start(out=hqT[:, ich * 1024 + qq * 256:
                                        ich * 1024 + qq * 256 + 256],
                                in_=bass.AP(E["hq_scr"], (qq * 256) * I_FFN + ich * 128,
                                            [[I_FFN, 256], [1, 128]]),
                                transpose=True).then_inc(s_ht.h, 16)
                    n += 16
        s_ht.take(16 * 4 * 86)

        for oe in range(8):
            for ich in range(86):
                t = oe * 86 + ich
                slot = t % 4

                @p.on("scalar")
                def _(e, oe=oe, ich=ich, t=t, slot=slot):
                    if t >= 4:
                        e.wait_ge(s_mm.h, t - 3)
                        e.wait_ge(s_wt[slot].h, 16 * (t // 4))
                    e.dma_start(out=wdt[:, slot * 512:(slot + 1) * 512],
                                in_=bass.AP(E["wdq_scr"], (oe * 512) * I_FFN + ich * 128,
                                            [[I_FFN, 512], [1, 128]]),
                                transpose=True).then_inc(s_wt[slot].h, 16)

                s_mm.take(1)

                @p.on("tensor")
                def _(e, oe=oe, ich=ich, t=t, slot=slot):
                    if oe == 0:
                        e.wait_ge(s_ht.h, 16 * 4 * (ich + 1))
                    if ich == 0 and oe:
                        e.wait_ge(s_ev.h, oe * 8)
                    e.wait_ge(s_wt[slot].h, 16 * (t // 4 + 1))
                    last = None
                    for c in range(NCH):
                        last = e.matmul(PS[c][:, 0:512],
                                        hqT[:, ich * 1024 + c * 128:ich * 1024 + c * 128 + 128],
                                        wdt[:, slot * 512:(slot + 1) * 512],
                                        start=(ich == 0), stop=(ich == 85))
                    last.then_inc(s_mm.h, 1)

            for c in range(NCH):
                e_i = oe * 8 + c
                sl = e_i % 2

                @p.on("scalar")
                def _(e, oe=oe, c=c, e_i=e_i, sl=sl):
                    if e_i:
                        e.wait_ge(s_r1.h, 16 * e_i)
                    if e_i >= 2:
                        e.wait_ge(s_ev.h, e_i - 1)
                    e.dma_start(out=r1l[:, sl * 512:sl * 512 + 512],
                                in_=bass.AP(E["r1_scr"], (c * 128) * HID + oe * 512,
                                            [[HID, 128], [1, 512]])).then_inc(s_r1.h, 16)

                @p.on("vector")
                def _(e, oe=oe, c=c, e_i=e_i, sl=sl):
                    e.wait_ge(s_mm.h, (oe + 1) * 86)
                    e.wait_ge(s_r1.h, 16 * (e_i + 1))
                    if e_i >= 2:
                        e.wait_ge(s_rd.h, 16 * (e_i - 1))
                    e.drain()
                    ts(e, x1p[:, :], r1l[:, sl * 512:sl * 512 + 512],
                       E["ln1_mean"][:, c:c + 1], Alu.subtract,
                       E["ln1_inv"][:, c:c + 1], Alu.mult)
                    e.drain()
                    e.tensor_tensor(r2e[:, sl * 512:sl * 512 + 512],
                                    PS[c][:, 0:512], x1p[:, :], Alu.add)
                    e.drain().then_inc(s_ev.h, 1)
                s_ev.take(1)

                @p.on("scalar")
                def _(e, oe=oe, c=c, e_i=e_i, sl=sl):
                    e.wait_ge(s_ev.h, e_i + 1)
                    if e_i:
                        e.wait_ge(s_rd.h, 16 * e_i)
                    e.dma_start(out=bass.AP(E["r2_scr"], (c * 128) * HID + oe * 512,
                                            [[HID, 128], [1, 512]]),
                                in_=r2e[:, sl * 512:sl * 512 + 512]).then_inc(s_rd.h, 16)
                s_r1.take(16)
                s_rd.take(16)

        rd_fin, ev_fin = s_rd.v, s_ev.v

        @p.on("scalar")
        def _(e):
            e.wait_ge(s_rd.h, rd_fin)

        @p.on("vector")
        def _(e):
            e.wait_ge(s_ev.h, ev_fin)
        p.run()

    _p45_ln2(nc, E)


def _p45_ln2(nc, E):
    with contextlib.ExitStack() as _st:
        r2row = _st.enter_context(nc.sbuf_tensor("p45_r2", [128, 2 * HID], F32))
        obuf = _st.enter_context(nc.sbuf_tensor("p45_ob", [128, HID], F32))
        tr = _st.enter_context(nc.sbuf_tensor("p45_tr", [128, HID], F32))
        mv = _st.enter_context(nc.sbuf_tensor("p45_mv", [128, 8], F32))

        p = Plan(nc, "p45")
        s_ld = p.sem("ld")
        s_sv = p.sem("sv")
        s_vv = p.sem("vv")
        s_ob = p.sem("ob")
        s_od = p.sem("od")

        for c in range(NCH):
            sl = c % 2

            @p.on("sync")
            def _(e, c=c, sl=sl):
                if c:
                    e.wait_ge(s_ld.h, 16 * c)
                if c >= 2:
                    e.wait_ge(s_ob.h, c - 1)
                e.dma_start(out=r2row[:, sl * HID:(sl + 1) * HID],
                            in_=bass.AP(E["r2_scr"], (c * 128) * HID,
                                        [[HID, 128], [1, HID]])).then_inc(s_ld.h, 16)

            svv = s_sv.take(2)

            @p.on("scalar")
            def _(e, c=c, sl=sl):
                e.wait_ge(s_ld.h, 16 * (c + 1))
                if c:
                    e.wait_ge(s_ob.h, c)
                slc = r2row[:, sl * HID:(sl + 1) * HID]
                e.activation(tr[:, :], slc, Act.Square, accum_out=mv[:, 0:1])
                e.drain()
                e.activation(tr[:, :], slc, Act.Identity, accum_out=mv[:, 1:2])
                e.drain()
                e.activation(mv[:, 2:3], mv[:, 1:2], Act.Identity, scale=E["actc"][:, 3:4])
                e.drain()
                e.activation(mv[:, 3:4], mv[:, 2:3], Act.Square)
                e.drain()
                e.activation(mv[:, 4:5], mv[:, 0:1], Act.Identity, scale=E["actc"][:, 3:4])
                e.drain().then_inc(s_sv.h, 1)

            vvv = s_vv.take(1)

            @p.on("vector")
            def _(e, svv=svv):
                e.wait_ge(s_sv.h, svv - 1)
                e.drain()
                e.tensor_sub(mv[:, 5:6], mv[:, 4:5], mv[:, 3:4])
                e.drain().then_inc(s_vv.h, 1)

            @p.on("scalar")
            def _(e, vvv=vvv):
                e.wait_ge(s_vv.h, vvv)
                e.activation(mv[:, 6:7], mv[:, 5:6], Act.Ln, bias=E["actc"][:, 1:2])
                e.drain()
                e.activation(mv[:, 7:8], mv[:, 6:7], Act.Exp, scale=-0.5)
                e.drain().then_inc(s_sv.h, 1)

            obv = s_ob.take(1)

            @p.on("vector")
            def _(e, c=c, sl=sl, svv=svv, obv=obv):
                e.wait_ge(s_sv.h, svv)
                if obv >= 2:
                    e.wait_ge(s_od.h, 16 * (obv - 1))
                e.drain()
                ts(e, obuf[:, :], r2row[:, sl * HID:(sl + 1) * HID], mv[:, 2:3],
                   Alu.subtract, mv[:, 7:8], Alu.mult)
                e.drain().then_inc(s_ob.h, 1)

            odv = s_od.take(16)

            @p.on("sync")
            def _(e, c=c, obv=obv, odv=odv):
                e.wait_ge(s_ob.h, obv)
                if odv >= 32:
                    e.wait_ge(s_od.h, odv - 16)
                e.dma_start(out=bass.AP(E["out"], c * 128 * HID, [[HID, 128], [1, HID]]),
                            in_=obuf[:, :]).then_inc(s_od.h, 16)

        od_fin = s_od.v

        @p.on("sync")
        def _(e):
            e.wait_ge(s_od.h, od_fin)

        @p.on("vector")
        def _(e):
            e.wait_ge(s_od.h, od_fin)
        p.run()


def _dbg_dump(nc, E):
    dbg = E["dbg"]
    p = Plan(nc, "dbgd")
    s = p.sem("st")
    sv = s.take(16 * sum(1 for nm in ["dbg_G", "dbg_state0", "dbg_dt"] if nm in dbg))

    @p.on("sync")
    def _(e):
        n = 0
        for nm, src in [("dbg_G", "G_tok"), ("dbg_state0", "state"), ("dbg_dt", "dt_tok")]:
            if nm in dbg:
                if n:
                    e.wait_ge(s.h, 16 * n)
                e.dma_start(out=dbg[nm][:, :], in_=E[src][:, :]).then_inc(s.h, 16)
                n += 1
        e.wait_ge(s.h, sv)

    s2 = p.sem("st2")
    sv2 = s2.take(16 * sum(1 for nm in ["dbg_ktok", "dbg_vtok"] if nm in dbg))

    @p.on("gpsimd")
    def _(e):
        n = 0
        for nm, src in [("dbg_ktok", "k_tok"), ("dbg_vtok", "v_tok")]:
            if nm in dbg:
                if n:
                    e.wait_ge(s2.h, 16 * n)
                e.dma_start(out=dbg[nm][:, :], in_=E[src][:, :]).then_inc(s2.h, 16)
                n += 1
        e.wait_ge(s2.h, sv2)

    p.run()



def _prep_inputs(inputs, core):
    b, q = core // 4, core % 4
    x = np.asarray(inputs["L0_attn_hidden_states"])[b, q * N_TOK:(q + 1) * N_TOK, :]
    d = dict(
        xT=np.ascontiguousarray(x.T),
        x_tok=np.ascontiguousarray(x),
        WqT=np.ascontiguousarray(np.asarray(inputs["L0_attn_Wq"]).T),
        WkT=np.ascontiguousarray(np.asarray(inputs["L0_attn_Wk"]).T),
        WvT=np.ascontiguousarray(np.asarray(inputs["L0_attn_Wv"]).T),
        WgT=np.ascontiguousarray(np.asarray(inputs["L0_attn_Wg"]).T),
        WoT=np.ascontiguousarray(np.asarray(inputs["L0_attn_Wo"]).T),
        WinT=np.ascontiguousarray(np.asarray(inputs["L0_attn_Win"]).T),
        Wg_f=np.ascontiguousarray(np.asarray(inputs["L0_ffn_w_gate"])),
        Wu_f=np.ascontiguousarray(np.asarray(inputs["L0_ffn_w_up"])),
        Wd_f=np.ascontiguousarray(np.asarray(inputs["L0_ffn_w_down"])),
        dtb_t=np.tile(np.asarray(inputs["L0_attn_dt_bias"])[None, :], (128, 1)),
        alog_t=np.tile(np.asarray(inputs["L0_attn_A_log_bias"])[None, :], (128, 1)),
        gnorm_t=np.tile(np.asarray(inputs["L0_attn_gnorm_w"])[None, :], (128, 1)),
    )
    d.update(_host_consts(core))
    return {k: np.ascontiguousarray(np.asarray(v, dtype=np.float32)) for k, v in d.items()}


def kernel(**inputs):
    nc, E = build_program(depth_stop=99, debug=0)
    nc.finalize()
    in_maps = [_prep_inputs(inputs, c) for c in range(N_CORES)]
    res = run_bass_kernel_spmd(nc, in_maps, core_ids=list(range(N_CORES)))
    shards = [np.asarray(r["out"]) for r in res.results]
    full = np.empty((B, L, HID), dtype=np.float32)
    for c in range(N_CORES):
        b, q = c // 4, c % 4
        full[b, q * N_TOK:(q + 1) * N_TOK, :] = shards[c]
    return full
